# version 01
# baseline (speedup 1.0000x reference)
import numpy as np
from contextlib import ExitStack
import concourse.bass as bass
import concourse.mybir as mybir
from concourse.alu_op_type import AluOpType

F32 = mybir.dt.float32
BF16 = mybir.dt.bfloat16
AF = mybir.ActivationFunctionType
MUL = AluOpType.mult
ADD = AluOpType.add
SUB = AluOpType.subtract

DM, SEQ, NST, DTR, DFF = 2048, 2048, 16, 128, 4096
G = DM // 128
TCA, TCB = 256, 512
NCA, NCB = SEQ // TCA, SEQ // TCB
QW, NDUO, DW = TCA + 2, 8, 2
JB, JF = DM // 128, DFF // 128
LNEPS = 1e-5
INV_D = 1.0 / DM

def _c_dve(fd, rate=1.074):
    return 100 + fd * rate


class Builder:
    def __init__(self, debug=False, phase="full", nca=NCA, scandrain=False,
                 snap=False):
        self.debug = debug
        self.phase = phase
        self.nca = nca
        self.scandrain = scandrain
        self.snap = snap
        self.nc = bass.Bass("TRN2", target_bir_lowering=False)
        self.es = ExitStack()
        self.dram = {}
        self.CNT = {}
        self.S = {}
        self.shadow = {k: [] for k in ["sync", "scalar", "vector", "pe", "gp"]}
        self.cur = None
        self.ev = []
        self.wp_idx = [0]
        self.bch_idx = [0]
        self.st_idx = [0]
        self.kp_idx = [0]
        self.ke_idx = [0]
        self.bpos = {}
        self.mode = "plan"
        self.planned = {k: [] for k in self.shadow}
        self.pptr = {k: 0 for k in self.shadow}
        self.piece_chain = {}
        self.op_end_bfree = {}
        self.w3_end_bps = {}
        self.ln_kevac = {}
        self.bbc_kps = {}

    DMASEMS = {"c", "cg", "r1w", "xn1w", "r2w", "outw", "ydbg", "hd"}

    def W(self, e, name, val):
        if self.mode == "plan":
            if name in self.DMASEMS:
                assert self.CNT[name] >= val, (name, val, self.CNT[name])
                val = self.CNT[name]
            self.planned[self.cur].append(val)
        else:
            val = self.planned[self.cur][self.pptr[self.cur]]
            self.pptr[self.cur] += 1
        if val > 0:
            e.wait_ge(self.S[name], val)
            if self.mode != "plan":
                self.shadow[self.cur].append(("w", name, val))

    def inc(self, name, instr, n=1):
        instr.then_inc(self.S[name], n)
        self.CNT[name] += n
        if self.mode != "plan":
            self.shadow[self.cur].append(("i", name, self.CNT[name], n))

    def cost(self, ns):
        if self.mode != "plan":
            self.shadow[self.cur].append(("c", ns))

    def dma(self, e, out, in_, sem, nbytes):
        instr = e.dma_start(out=out, in_=in_)
        instr.then_inc(self.S[sem], 16)
        self.CNT[sem] += 16
        if self.mode != "plan":
            self.shadow[self.cur].append(("d", sem, self.CNT[sem], nbytes))
        return instr

    def E_(self, eng, fn):
        self.ev.append((eng, fn))

    def at(self, pos, eng, fn):
        self.bpos.setdefault(pos, []).append((eng, fn))

    def build(self):
        nc, es = self.nc, self.es
        debug = self.debug

        def din(name, shape, dt=F32):
            self.dram[name] = nc.dram_tensor(name, shape, dt,
                                             kind="ExternalInput").ap()
            return self.dram[name]

        xT = din("xT", [DM, SEQ])
        xpdt = din("xpdt", [DM, DTR])
        xpbc = din("xpbc", [DM, 2 * NST])
        dtwT = din("dtwT", [DTR, DM])
        for nm, sh in [("dtb_g", [128, G]), ("alog_g", [128, G * NST]),
                       ("dd_g", [128, G]), ("opb_g", [128, G]),
                       ("b1_g", [128, JF]), ("b2_g", [128, JF]),
                       ("b3_g", [128, G])]:
            din(nm, sh)
        wsrc = {"op": din("wop", [JB, DM, 128]),
                "w1": din("w1p", [JF, DM, 128]),
                "w2": din("w2p", [JF, DFF, 128]),
                "w3": din("w3p", [G, DFF, 128])}
        KTs = {"op": G, "w1": G, "w2": JF, "w3": JF}

        outT = nc.dram_tensor("outT", [DM, SEQ], F32, kind="ExternalOutput").ap()
        self.outT = outT
        skind = "ExternalOutput" if debug else "Internal"
        r1s = nc.dram_tensor("r1s", [DM, SEQ], F32, kind=skind).ap()
        xn1s = nc.dram_tensor("xn1s", [DM, SEQ], F32, kind=skind).ap()
        r2s = nc.dram_tensor("r2s", [DM, SEQ], F32, kind=skind).ap()
        if debug:
            y_dbg = nc.dram_tensor("y_dbg", [DM, SEQ], BF16,
                                   kind="ExternalOutput").ap()

        def sbuf(name, shape, dt):
            return es.enter_context(nc.sbuf_tensor(name, shape, dt))

        xpdt_bf = sbuf("xpdt_bf", [128, G, 128], BF16)
        xpbc_bf = sbuf("xpbc_bf", [128, G, 32], BF16)
        dtwT_bf = sbuf("dtwT_bf", [128, DM], BF16)
        dtb_sb = sbuf("dtb_sb", [128, G], F32)
        A_sb = sbuf("A_sb", [128, G * NST], F32)
        oneD = sbuf("oneD", [128, G], F32)
        opb_sb = sbuf("opb_sb", [128, G], F32)
        b1_sb = sbuf("b1_sb", [128, JF], F32)
        b2_sb = sbuf("b2_sb", [128, JF], F32)
        b3_sb = sbuf("b3_sb", [128, G], F32)
        ones1 = sbuf("ones1", [1, 128], BF16)
        ones_red = sbuf("ones_red", [128, 1], BF16)
        cst = sbuf("cst", [128, 3], F32)
        spt = sbuf("spt", [128, 2, 2, TCA], F32)
        cst1 = sbuf("cst1", [1, 2], F32)
        xmm = sbuf("xmm", [128, 3, TCA], BF16)
        xdu = sbuf("xdu", [128, 3, TCA], BF16)
        dtd_bf = sbuf("dtd_bf", [128, 2, TCA], BF16)
        bc_bf = sbuf("bc_bf", [32, 2, TCA], BF16)
        bc_row = sbuf("bc_row", [1, 2, 4, TCA], BF16)
        Bb = sbuf("Bb", [128, NST, TCA], BF16)
        Cb = sbuf("Cb", [128, NST, TCA], BF16)
        delta = sbuf("delta", [128, 4, 8, TCA], BF16)
        Et = sbuf("Et", [128, 2, DW * QW], F32)
        Ut = sbuf("Ut", [128, DW * QW], BF16)
        Ht = sbuf("Ht", [128, DW * QW], BF16)
        tmp = sbuf("tmp", [128, NST, TCA], BF16)
        du_t = sbuf("du_t", [128, TCA], BF16)
        Hcar = sbuf("Hcar", [128, DW, G * NDUO], BF16)
        HcarD = sbuf("HcarD", [128, DW, G * NDUO], BF16)
        yb = sbuf("yb", [128, 3, G, TCA], BF16)
        rbf2 = sbuf("rbf2", [128, 1, 2, TCB], BF16)
        stw = sbuf("stw", [1, 2, TCB], F32)
        stbf = sbuf("stbf", [1, 2, TCB], BF16)
        mbt = sbuf("mbt", [128, TCB], BF16)
        sbt = sbuf("sbt", [128, TCB], BF16)
        r1t = sbuf("r1t", [128, 3, TCB], F32)
        opf = sbuf("opf", [128, 2, TCB], F32)
        xBt = sbuf("xBt", [128, 2, TCB], F32)
        xn1bf = sbuf("xn1bf", [128, G, TCB], BF16)
        h1 = sbuf("h1", [128, JF, TCB], BF16)
        h2b = sbuf("h2b", [128, G, TCB], BF16)
        f3t = sbuf("f3t", [128, 2, TCB], F32)
        xn1r = sbuf("xn1r", [128, 2, TCB], F32)
        rrt = sbuf("rrt", [128, 2, TCB], F32)
        wsl = sbuf("wsl", [128, 3, G, 128], BF16)

        ps_a = es.enter_context(nc.psum_tensor("ps_a", [128, 2 * TCA], F32))
        ps_bc = es.enter_context(nc.psum_tensor("ps_bc", [32, TCA], F32))
        ps_st = es.enter_context(nc.psum_tensor("ps_st", [1, 2 * TCB], F32))
        ps_k = [es.enter_context(nc.psum_tensor(f"ps_k{i}", [128, TCB], F32))
                for i in range(2)]
        ps_b = [es.enter_context(nc.psum_tensor(f"ps_b{i}", [128, TCB], F32))
                for i in range(2)]
        ps_dtdA = ps_a[:, 0:TCA]
        ps_dtTA = [ps_a[:, 0:TCA], ps_a[:, TCA:2 * TCA]]

        semnames = (
            ["c", "cg", "init", "xfree", "du", "dtdps", "bcps",
             "dtdbf", "bcbf", "dtT", "sp", "spc", "dl", "kps",
             "kevac", "bbfree", "E", "scan", "y", "Bps",
             "Bfree", "r1", "r1w", "rbf", "stmm", "stps", "lnm",
             "rstd", "xn1f", "xn1w", "xn1bf", "r2", "r2w",
             "outv", "outw", "ydbg", "hd"]
            + [f"xmm{i}" for i in range(3)] + [f"xdu{i}" for i in range(3)]
            + [f"wp{i}" for i in range(3)] + [f"xB{i}" for i in range(2)]
            + [f"r1r{i}" for i in range(2)] + [f"xn1rd{i}" for i in range(2)]
            + [f"r2r{i}" for i in range(2)] + [f"bcrow{i}" for i in range(2)]
        )
        self.S = {n: es.enter_context(nc.semaphore("s_" + n))
                  for n in semnames}
        self.CNT = {n: 0 for n in semnames}
        W, inc, dma, cost = self.W, self.inc, self.dma, self.cost
        S, CNT = self.S, self.CNT
        E_, at = self.E_, self.at

        def st_sync(e):
            for nm, dst in [("dtb_g", dtb_sb), ("alog_g", A_sb),
                            ("dd_g", oneD), ("opb_g", opb_sb),
                            ("b1_g", b1_sb), ("b2_g", b2_sb),
                            ("b3_g", b3_sb)]:
                dma(e, dst[:], self.dram[nm], "c", 128 * 64 * 4)
        E_("sync", st_sync)

        def st_gp(e):
            dma(e, xpdt_bf[:], xpdt.rearrange("(g p) m -> p g m", p=128),
                "cg", DM * DTR * 4)
            dma(e, xpbc_bf[:], xpbc.rearrange("(g p) m -> p g m", p=128),
                "cg", DM * 32 * 4)
            dma(e, dtwT_bf[:], dtwT, "cg", DTR * DM * 4)
        E_("gp", st_gp)

        def st_vec0(e):
            e.memset(ones1[:], 1.0)
            e.memset(ones_red[:], 1.0)
            e.memset(Hcar[:], 0.0)
            e.memset(cst[:, 0:1], 1.0)
            e.memset(cst[:, 1:2], LNEPS)
            e.memset(cst[:, 2:3], -1.0)
            e.memset(cst1[:, 0:1], LNEPS)
            e.memset(cst1[:, 1:2], -0.5)
            u3 = Ut[:].rearrange("p (n w) -> p n w", n=DW)
            e.memset(u3[:, :, 0:2], 0.0)
            for s in range(2):
                e3 = Et[:, s, :].rearrange("p (n w) -> p n w", n=DW)
                m = e.memset(e3[:, :, 0:2], 0.0)
            inc("init", m)
            cost(2000)
        E_("vector", st_vec0)

        def st_gp0(e):
            for g in range(3):
                dma(e, xdu[:, g, :], xT[g * 128:(g + 1) * 128, 0:TCA],
                    f"xdu{g}", 128 * TCA * 4)
        E_("gp", st_gp0)

        def st_scalar(e):
            W(e, "c", 112)
            W(e, "cg", 48)
            W(e, "init", 1)
            inc("init", e.activation(out=A_sb[:], in_=A_sb[:], func=AF.Exp))
            cost(1000)
        E_("scalar", st_scalar)

        def st_vec(e):
            W(e, "init", 2)
            e.tensor_scalar_mul(out=A_sb[:], in0=A_sb[:], scalar1=-1.0)
            W(e, "c", 112)
            inc("init", e.tensor_scalar(out=oneD[:], in0=oneD[:], scalar1=1.0,
                                        scalar2=None, op0=ADD))
            cost(500)
        E_("vector", st_vec)

        def a_xload(c, g):
            i = c * G + g

            def f(e):
                if i >= 3:
                    ct, gt = divmod(i - 3, G)
                    if gt >= 14:
                        W(e, "bcps", ct + 1)
                    else:
                        W(e, "xfree", 7 * ct + gt // 2 + 1)
                dma(e, xmm[:, i % 3, :],
                    xT[g * 128:(g + 1) * 128, c * TCA:(c + 1) * TCA],
                    f"xmm{i % 3}", 128 * TCA * 4)
            return f

        def a_mm(c):
            def f(e):
                if c == 0:
                    W(e, "c", 112)
                    W(e, "cg", 48)
                    W(e, "init", 1)
                if c >= 1:
                    W(e, "sp", G * c - 1)
                if c >= 2:
                    W(e, "dtT", G * (c - 1))
                    W(e, "bcbf", 0)
                for g in range(G):
                    i = c * G + g
                    W(e, f"xmm{i % 3}", 16 * (i // 3 + 1))
                    md = e.matmul(out=ps_dtdA, lhsT=xpdt_bf[:, g, :],
                                  rhs=xmm[:, i % 3, :], start=(g == 0),
                                  stop=(g == G - 1))
                    m = e.matmul(out=ps_bc[:], lhsT=xpbc_bf[:, g, :],
                                 rhs=xmm[:, i % 3, :], start=(g == 0),
                                 stop=(g == G - 1))
                    if g == G - 1:
                        inc("dtdps", md)
                        inc("bcps", m)
                    elif g % 2 == 1:
                        inc("xfree", m)
                    cost(2 * TCA * 0.42 + 60)
            return f

        def a_evac(c):
            def f(e):
                W(e, "dtdps", c + 1)
                if c >= 2:
                    W(e, "dtT", G * (c - 1))
                inc("dtdbf", e.activation(out=dtd_bf[:, c % 2, :],
                                          in_=ps_dtdA, func=AF.Copy))
                cost(600)
                W(e, "bcps", c + 1)
                if c >= 2:
                    Bp = 8 * (c - 2) + 7
                    W(e, f"bcrow{Bp % 2}", 16 * (Bp // 2 + 1))
                inc("bcbf", e.activation(out=bc_bf[:, c % 2, :],
                                         in_=ps_bc[:], func=AF.Copy))
                cost(600)
            return f

        def a_dtTg(c, g):
            def f(e):
                if g == 0:
                    W(e, "dtdbf", c + 1)
                if g >= 2:
                    W(e, "sp", G * c + g - 1)
                inc("dtT", e.matmul(
                    out=ps_dtTA[g % 2],
                    lhsT=dtwT_bf[:, g * 128:(g + 1) * 128],
                    rhs=dtd_bf[:, c % 2, :], start=True, stop=True))
                cost(TCA * 0.42 + 120)
            return f

        def a_spg(c, g):
            i = c * G + g

            def f(e):
                W(e, "dtT", G * c + g + 1)
                if i >= 2:
                    W(e, "dl", i - 1)
                e.activation(out=spt[:, i % 2, 1, :], in_=ps_dtTA[g % 2],
                             func=AF.Relu, bias=dtb_sb[:, g:g + 1])
                cost(520)
                inc("sp", e.activation(out=spt[:, i % 2, 0, :],
                                       in_=ps_dtTA[g % 2],
                                       func=AF.Abs, bias=dtb_sb[:, g:g + 1]))
                e.drain()
                cost(700)
                e.activation(out=spt[:, i % 2, 0, :],
                             in_=spt[:, i % 2, 0, :], func=AF.Exp,
                             scale=cst[:, 2:3])
                e.drain()
                cost(700)
                a = e.activation(out=spt[:, i % 2, 0, :],
                                 in_=spt[:, i % 2, 0, :], func=AF.Ln,
                                 bias=cst[:, 0:1])
                cost(520)
                inc("spc", a)
            return f

        def a_dladd(c, g):
            i = c * G + g
            H = 2 * c + g // 8

            def f(e):
                W(e, "spc", i + 1)
                inc("dl", e.tensor_tensor(out=delta[:, H % 4, g % 8, :],
                                          in0=spt[:, i % 2, 0, :],
                                          in1=spt[:, i % 2, 1, :], op=ADD))
                cost(_c_dve(TCA))
            return f

        def a_bcflat(c, h):
            B = 8 * c + h

            def fs(e):
                if B >= 2:
                    W(e, "kps", self.bbc_kps[B - 2])
                dma(e, bc_row[0:1, h % 2, :, :],
                    bc_bf[h * 4:(h + 1) * 4, c % 2, :],
                    f"bcrow{B % 2}", 4 * TCA * 2)

            def fp(e):
                if h == 0:
                    W(e, "bcbf", c + 1)
                    if c == 0:
                        W(e, "init", 1)
                    if c >= 1:
                        W(e, "bbfree", c)
                W(e, f"bcrow{B % 2}", 16 * (B // 2 + 1))
                for half in range(2):
                    ki = self.kp_idx[0]
                    self.kp_idx[0] += 1
                    if ki >= 2:
                        W(e, "kevac", ki - 1)
                    for cc in range(2):
                        col = h * 4 + half * 2 + cc
                        m = e.matmul(
                            out=ps_k[ki % 2][:, cc * TCA:(cc + 1) * TCA],
                            lhsT=ones1[:],
                            rhs=bc_row[0:1, h % 2, col % 4, :],
                            start=True, stop=True)
                        cost(TCA * 0.42 + 60)
                    inc("kps", m)
                self.bbc_kps[B] = self.CNT["kps"]

            def fe(e):
                for half in range(2):
                    col = h * 4 + half * 2
                    ki = self.ke_idx[0]
                    self.ke_idx[0] += 1
                    W(e, "kps", ki + 1)
                    dst = Bb if col < NST else Cb
                    inc("kevac", e.activation(
                        out=dst[:, col % NST:col % NST + 2, :],
                        in_=ps_k[ki % 2][:], func=AF.Copy))
                    cost(940)
            return fs, fp, fe

        def emit_bcflat(c):
            for h in range(8):
                fs, fp, fe = a_bcflat(c, h)
                E_("scalar", fs)
                E_("pe", fp)
                E_("scalar", fe)

        def a_du(c, g):
            i = c * G + g

            def f(e):
                W(e, "sp", G * c + g + 1)
                W(e, f"xdu{i % 3}", 16 * (i // 3 + 1))
                if g == 0:
                    W(e, "kevac", 16 * (c + 1))
                e.tensor_tensor(out=du_t[:],
                                in0=delta[:, (2 * c + g // 8) % 4, g % 8, :],
                                in1=xdu[:, i % 3, :], op=MUL)
                cost(_c_dve(TCA, 0.567))
            return f

        def a_E(c, g, q):
            eidx = (c * G + g) * NDUO + q

            def f(e):
                if eidx == 0:
                    W(e, "init", 3)
                if q == 0:
                    W(e, "dl", c * G + g + 1)
                if eidx >= 2:
                    W(e, "scan", eidx - 1)
                for nn in range(DW):
                    n = q * DW + nn
                    a = e.activation(
                        out=Et[:, eidx % 2, nn * QW + 2:nn * QW + 2 + TCA],
                        in_=delta[:, (2 * c + g // 8) % 4, g % 8, :], func=AF.Exp,
                        scale=A_sb[:, g * NST + n:g * NST + n + 1])
                    cost(520)
                    if nn == DW - 1:
                        inc("E", a)
            return f

        def a_xduload(c, g):
            i2 = c * G + g + 2

            def f(e):
                c2, g2 = divmod(i2, G)
                W(e, "du", i2 - 2)
                dma(e, xdu[:, i2 % 3, :],
                    xT[g2 * 128:(g2 + 1) * 128, c2 * TCA:(c2 + 1) * TCA],
                    f"xdu{i2 % 3}", 128 * TCA * 4)
            return f

        u3v = Ut[:].rearrange("p (n w) -> p n w", n=DW)
        h3v = Ht[:].rearrange("p (n w) -> p n w", n=DW)

        def a_scanq(c, g, q):
            eidx = (c * G + g) * NDUO + q

            def f(e):
                if c >= 1 and q == 0:
                    W(e, "hd", 16)
                gd = g * NDUO + q
                csrc = (Hcar if c == 0 else HcarD)[:, :, gd:gd + 1]
                e.tensor_copy(out=u3v[:, :, 1:2], in_=csrc)
                cost(160)
                for nn in range(DW):
                    n = q * DW + nn
                    e.tensor_tensor(out=u3v[:, nn, 2:QW], in0=du_t[:],
                                    in1=Bb[:, n, :], op=MUL)
                    cost(_c_dve(TCA, 0.567))
                if eidx >= 1:
                    pg, pq = divmod((eidx - 1) % (G * NDUO), NDUO)
                    e.tensor_copy(out=Hcar[:, :, pg * NDUO + pq:
                                           pg * NDUO + pq + 1],
                                  in_=h3v[:, :, QW - 1:QW])
                    cost(160)
                W(e, "E", eidx + 1)
                if self.scandrain:
                    e.drain()
                    cost(100)
                inc("scan", e.tensor_tensor_scan(
                    out=Ht[:], data0=Et[:, eidx % 2, :], data1=Ut[:],
                    initial=0.0, op0=MUL, op1=ADD))
                cost(330 + DW * QW * 2.13)
                if self.snap and c == 1 and g == 0:
                    hf = h1[:].rearrange("p a b -> p (a b)")
                    e.tensor_copy(out=hf[:, q * 1548:q * 1548 + 516],
                                  in_=Ht[:])
                    e.tensor_copy(out=hf[:, q * 1548 + 516:q * 1548 + 1032],
                                  in_=Ut[:])
                    e.tensor_copy(out=hf[:, q * 1548 + 1032:q * 1548 + 1548],
                                  in_=Et[:, eidx % 2, :])
                    cost(900)

                e.tensor_tensor(out=tmp[:, q * DW:(q + 1) * DW, :],
                                in0=h3v[:, :, 2:QW],
                                in1=Cb[:, q * DW:(q + 1) * DW, :], op=MUL)
                cost(_c_dve(DW * TCA, 0.567))
            return f

        t4 = tmp[:].rearrange("p (a b) t -> p a b t", b=2)

        def a_tree(c, g):
            def f(e):
                if self.phase == "full" and c >= 3 and g == 0:
                    W(e, "Bfree", self.op_end_bfree[(c - 3) // 2])
                inc("du", e.tensor_tensor(out=tmp[:, 0:8, :],
                                          in0=t4[:, :, 0, :],
                                          in1=t4[:, :, 1, :], op=ADD))
                cost(_c_dve(8 * TCA, 0.567))
                e.tensor_tensor(out=tmp[:, 0:4, :], in0=t4[:, 0:4, 0, :],
                                in1=t4[:, 0:4, 1, :], op=ADD)
                cost(_c_dve(4 * TCA, 0.567))
                m3 = e.tensor_tensor(out=tmp[:, 0:2, :], in0=t4[:, 0:2, 0, :],
                                     in1=t4[:, 0:2, 1, :], op=ADD)
                cost(_c_dve(2 * TCA, 0.567))
                if g == G - 1:
                    inc("bbfree", m3)
                inc("y", e.tensor_tensor(out=yb[:, c % 3, g, :],
                                         in0=tmp[:, 0, :], in1=tmp[:, 1, :],
                                         op=ADD))
                cost(_c_dve(TCA, 0.567))
            return f

        def a_hdma(c2, g2):
            def f(e):
                W(e, "scan", 128 * c2 + 8 * (g2 + 2))
                dma(e, HcarD[:, :, g2 * NDUO:(g2 + 1) * NDUO],
                    Hcar[:, :, g2 * NDUO:(g2 + 1) * NDUO], "hd",
                    128 * NST * 2)
            return f

        def a_ydbg(c):
            def f(e):
                W(e, "y", G * (c + 1))
                dma(e, y_dbg[:, c * TCA:(c + 1) * TCA]
                    .rearrange("(g p) t -> p g t", p=128),
                    yb[:, c % 3, :, :], "ydbg", DM * TCA * 2)
            return f

        def b_wload(stage, j):
            KT = KTs[stage]
            npc = KT // G
            p0 = self.wp_idx[0]
            self.wp_idx[0] += npc

            def f(e):
                src3 = wsrc[stage][j].rearrange("(h p) m -> p h m", p=128)
                for pc in range(npc):
                    pidx = p0 + pc
                    if pidx >= 3:
                        W(e, "Bfree", self.piece_chain[pidx - 3])
                    dma(e, wsl[:, pidx % 3, :, :],
                        src3[:, pc * G:(pc + 1) * G, :],
                        f"wp{pidx % 3}", G * 128 * 128 * 4)
            return f, (p0, npc)

        def b_chain(C, stage, j, pinfo, first_wait=None):
            bidx = self.bch_idx[0]
            self.bch_idx[0] += 1
            KT = KTs[stage]
            p0, npc = pinfo

            def f(e):
                if first_wait is not None:
                    W(e, *first_wait)
                for pc in range(npc):
                    W(e, f"wp{(p0 + pc) % 3}", 16 * ((p0 + pc) // 3 + 1))
                if bidx >= 2:
                    W(e, "Bfree", bidx - 1)
                if stage == "op":
                    for half in range(2):
                        for k in range(KT):
                            lhsT = wsl[:, (p0 + k // G) % 3, k % G, :]
                            m = e.matmul(
                                out=ps_b[bidx % 2][:, half * TCA:
                                                   (half + 1) * TCA],
                                lhsT=lhsT,
                                rhs=yb[:, (2 * C + half) % 3, k, :],
                                start=(k == 0), stop=(k == KT - 1))
                            cost(TCA * 0.42 + 120)
                for k in (range(0) if stage == "op" else range(KT)):
                    lhsT = wsl[:, (p0 + k // G) % 3, k % G, :]
                    if False:
                        pass
                    else:
                        rhs = {"w1": lambda: xn1bf[:, k, :],
                               "w2": lambda: h1[:, k, :],
                               "w3": lambda: (xn1bf[:, k, :] if k < G
                                              else h2b[:, k - G, :])}[stage]()
                        m = e.matmul(out=ps_b[bidx % 2][:], lhsT=lhsT, rhs=rhs,
                                     start=(k == 0), stop=(k == KT - 1))
                        cost(TCB * 0.42 + 60)
                inc("Bps", m)
                for pc in range(npc):
                    self.piece_chain[p0 + pc] = bidx + 1
            return f, bidx

        def b_evac(C, stage, j, bidx):
            def f(e):
                W(e, "Bps", bidx + 1)
                if stage == "op":
                    if C * G + j >= 2:
                        W(e, "r1", C * G + j - 1)
                    a = e.activation(out=opf[:, j % 2, :],
                                     in_=ps_b[bidx % 2][:],
                                     func=AF.Relu,
                                     bias=opb_sb[:, j:j + 1])
                elif stage == "w1":
                    a = e.activation(out=h1[:, j, :], in_=ps_b[bidx % 2][:],
                                     func=AF.Relu, bias=b1_sb[:, j:j + 1])
                elif stage == "w2":
                    dst = xn1bf[:, j, :] if j < G else h2b[:, j - G, :]
                    a = e.activation(out=dst, in_=ps_b[bidx % 2][:],
                                     func=AF.Relu, bias=b2_sb[:, j:j + 1])
                else:
                    if C * G + j >= 2:
                        W(e, "r2", C * G + j - 1)
                    a = e.activation(out=f3t[:, j % 2, :],
                                     in_=ps_b[bidx % 2][:],
                                     func=AF.Identity,
                                     bias=b3_sb[:, j:j + 1])

                cost(780)
                inc("Bfree", a)
                if stage == "op" and j == JB - 1:
                    self.op_end_bfree[C] = self.CNT["Bfree"]
            return f

        def b_xb_prime(C):
            def f(e):
                for j in range(2):
                    i = C * G + j
                    if i >= 2:
                        W(e, "r1", i - 1)
                    dma(e, xBt[:, i % 2, :],
                        xT[j * 128:(j + 1) * 128, C * TCB:(C + 1) * TCB],
                        f"xB{i % 2}", 128 * TCB * 4)
            return f

        def r1t_consumers(u):
            Cp, jj = divmod(u, 32)
            if jj < G:
                return [("r1w", 16 * (G * Cp + jj + 1)),
                        ("rbf", 32 * Cp + jj + 1)]
            jj -= G
            return [("r2w", 16 * (G * Cp + jj + 1)),
                    ("rbf", 32 * Cp + G + jj + 1)]

        def b_stt(C, j, opf_bfree):
            i = C * G + j
            u = 32 * C + j

            def f(e):
                W(e, "Bfree", opf_bfree[0])
                W(e, f"xB{i % 2}", 16 * (i // 2 + 1))
                if u >= 3:
                    for nm, v in r1t_consumers(u - 3):
                        W(e, nm, v)
                inc("r1", e.scalar_tensor_tensor(
                    out=r1t[:, u % 3, :], in0=xBt[:, i % 2, :],
                    scalar=oneD[:, j:j + 1], in1=opf[:, j % 2, :],
                    op0=MUL, op1=ADD))
                cost(_c_dve(TCB, 1.09))
            return f

        def b_r1w(C, j):
            def f(e):
                W(e, "r1", C * G + j + 1)
                dma(e, r1s[j * 128:(j + 1) * 128, C * TCB:(C + 1) * TCB],
                    r1t[:, (32 * C + j) % 3, :], "r1w", 128 * TCB * 4)
            return f

        def b_rbf(C, ln, j):
            ridx = self.st_idx[0]
            self.st_idx[0] += 1
            u = 32 * C + ln * G + j

            def f(e):
                W(e, "r1" if ln == 0 else "r2", C * G + j + 1)
                if ln == 1 and j + 2 < G:
                    W(e, "xn1w", 16 * (C * G + j + 3))
                    dma(e, xn1r[:, j % 2, :],
                        xn1s[(j + 2) * 128:(j + 3) * 128,
                             C * TCB:(C + 1) * TCB],
                        f"xn1rd{(C * G + j + 2) % 2}", 128 * TCB * 4)
                if ln == 0 and j + 2 < G:
                    dma(e, xBt[:, (C * G + j + 2) % 2, :],
                        xT[(j + 2) * 128:(j + 3) * 128,
                           C * TCB:(C + 1) * TCB],
                        f"xB{(C * G + j + 2) % 2}", 128 * TCB * 4)
                if ridx >= 1:
                    full, rem = divmod(ridx, G)
                    W(e, "stmm", full * (G - 1) + max(0, rem - 1)
                      + (1 if rem else 0))
                    if rem == 0 and full >= 1:
                        W(e, "stps", full)
                e.activation(out=rbf2[:, 0, 0, :],
                             in_=r1t[:, u % 3, :], func=AF.Copy)
                cost(780)
                inc("rbf", e.activation(out=rbf2[:, 0, 1, :],
                                        in_=r1t[:, u % 3, :], func=AF.Square))
                cost(780)
            return f, ridx

        def b_statmm(L, j, ridx):
            def f(e):
                W(e, "rbf", ridx + 1)
                if j == 0 and L >= 1:
                    W(e, "lnm", L)
                e.matmul(out=ps_st[0:1, 0:TCB], lhsT=ones_red[:],
                         rhs=rbf2[:, 0, 0, :],
                         start=(j == 0), stop=(j == G - 1))
                m = e.matmul(
                    out=ps_st[0:1, TCB:2 * TCB], lhsT=ones_red[:],
                    rhs=rbf2[:, 0, 1, :],
                    start=(j == 0), stop=(j == G - 1))
                cost(2 * TCB * 0.42 + 120)
                if j == G - 1:
                    inc("stps", m)
                else:
                    inc("stmm", m)
            return f

        def b_lnm_v(L):
            def f(e):
                W(e, "stps", L + 1)
                e.tensor_scalar_mul(out=stw[:, 0, :], in0=ps_st[0:1, 0:TCB],
                                    scalar1=INV_D)
                e.tensor_scalar_mul(out=stw[:, 1, :],
                                    in0=ps_st[0:1, TCB:2 * TCB],
                                    scalar1=INV_D)
                e.tensor_tensor(out=ps_st[0:1, 0:TCB], in0=stw[:, 0, :],
                                in1=stw[:, 0, :], op=MUL)
                inc("lnm", e.tensor_tensor(out=stw[:, 1, :], in0=stw[:, 1, :],
                                           in1=ps_st[0:1, 0:TCB], op=SUB))
                cost(4000)
            return f

        def b_lnm_s2(L):
            def f(e):
                W(e, "lnm", L + 1)
                e.activation(out=stw[:, 1, :], in_=stw[:, 1, :], func=AF.Ln,
                             bias=cst1[:, 0:1])
                e.drain()
                e.activation(out=stw[:, 1, :], in_=stw[:, 1, :], func=AF.Exp,
                             scale=cst1[:, 1:2])
                e.drain()
                e.activation(out=stbf[:, 0, :], in_=stw[:, 0, :], func=AF.Copy)
                inc("rstd", e.activation(out=stbf[:, 1, :], in_=stw[:, 1, :],
                                         func=AF.Copy))
                cost(3600)
            return f

        def b_lnm_pe(L):
            def f(e):
                W(e, "rstd", L + 1)
                for i in range(2):
                    ki = self.kp_idx[0]
                    self.kp_idx[0] += 1
                    if ki >= 2:
                        W(e, "kevac", ki - 1)
                    inc("kps", e.matmul(out=ps_k[ki % 2][:],
                                        lhsT=ones1[:],
                                        rhs=stbf[0:1, i, :],
                                        start=True, stop=True))
                    cost(TCB * 0.42 * 4 + 60)
            return f

        def b_lnm_s3(L):
            def f(e):
                if L >= 1:
                    Cp, lnp = divmod(L - 1, 2)
                    if lnp == 0:
                        W(e, "xn1f", G * (Cp + 1))
                    else:
                        W(e, "outv", G * (Cp + 1))
                for i, dst in enumerate([mbt, sbt]):
                    ki = self.ke_idx[0]
                    self.ke_idx[0] += 1
                    W(e, "kps", ki + 1)
                    a = e.activation(out=dst[:], in_=ps_k[ki % 2][:],
                                     func=AF.Copy)
                    inc("kevac", a)
                    cost(780)
                self.ln_kevac[L] = self.CNT["kevac"]
            return f

        def b_r1r_prime(C):
            def f(e):
                if C >= 1:
                    W(e, "outw", 16 * G * C)
                for j in range(2):
                    i = C * G + j
                    W(e, "r1w", 16 * (i + 1))
                    dma(e, rrt[:, j % 2, :],
                        r1s[j * 128:(j + 1) * 128, C * TCB:(C + 1) * TCB],
                        f"r1r{i % 2}", 128 * TCB * 4)
            return f

        def b_apply1(C, j):
            i = C * G + j

            def f(e):
                W(e, "kevac", self.ln_kevac[2 * C])
                W(e, f"r1r{i % 2}", 16 * (i // 2 + 1))
                e.tensor_tensor(out=rrt[:, j % 2, :], in0=rrt[:, j % 2, :],
                                in1=mbt[:], op=SUB)
                cost(_c_dve(TCB))
                inc("xn1f", e.tensor_tensor(out=rrt[:, j % 2, :],
                                            in0=rrt[:, j % 2, :],
                                            in1=sbt[:], op=MUL))
                cost(_c_dve(TCB))
            return f

        def b_xn1w(C, j):
            def f(e):
                W(e, "xn1f", C * G + j + 1)
                dma(e, xn1s[j * 128:(j + 1) * 128, C * TCB:(C + 1) * TCB],
                    rrt[:, j % 2, :], "xn1w", 128 * TCB * 4)
            return f

        def b_xn1bf(C, j, w3_prev_bps):
            def f(e):
                W(e, "xn1f", C * G + j + 1)
                if w3_prev_bps is not None and j == 0:
                    W(e, "Bps", w3_prev_bps)
                inc("xn1bf", e.activation(out=xn1bf[:, j, :],
                                          in_=rrt[:, j % 2, :],
                                          func=AF.Copy))
                cost(780)
                j2 = j + 2
                if j2 < G:
                    i2 = C * G + j2
                    W(e, "r1w", 16 * (i2 + 1))
                    W(e, "xn1w", 16 * (C * G + j + 1))
                    dma(e, rrt[:, j2 % 2, :],
                        r1s[j2 * 128:(j2 + 1) * 128, C * TCB:(C + 1) * TCB],
                        f"r1r{i2 % 2}", 128 * TCB * 4)
            return f

        def b_xn1r_prime(C):
            def f(e):
                for j in range(2):
                    W(e, "xn1w", 16 * (C * G + j + 1))
                    dma(e, xn1r[:, j % 2, :],
                        xn1s[j * 128:(j + 1) * 128, C * TCB:(C + 1) * TCB],
                        f"xn1rd{(C * G + j) % 2}", 128 * TCB * 4)
            return f

        def b_r2(C, j, f3_bfree):
            i = C * G + j
            u = 32 * C + G + j

            def f(e):
                W(e, "Bfree", f3_bfree[0])
                W(e, f"xn1rd{i % 2}", 16 * (i // 2 + 1))
                for nm, v in r1t_consumers(u - 3):
                    W(e, nm, v)
                inc("r2", e.tensor_tensor(out=r1t[:, u % 3, :],
                                          in0=f3t[:, j % 2, :],
                                          in1=xn1r[:, j % 2, :], op=ADD))
                cost(_c_dve(TCB))
            return f

        def b_r2w(C, j):
            def f(e):
                W(e, "r2", C * G + j + 1)
                dma(e, r2s[j * 128:(j + 1) * 128, C * TCB:(C + 1) * TCB],
                    r1t[:, (32 * C + G + j) % 3, :], "r2w", 128 * TCB * 4)
            return f

        def b_r2r_loads(C):
            def f(e):
                W(e, "xn1w", 16 * G * (C + 1))
                W(e, "xn1bf", G * (C + 1))
                for j in range(2):
                    i = C * G + j
                    W(e, "r2w", 16 * (i + 1))
                    dma(e, rrt[:, j % 2, :],
                        r2s[j * 128:(j + 1) * 128, C * TCB:(C + 1) * TCB],
                        f"r2r{i % 2}", 128 * TCB * 4)
            return f

        def b_r2r_ld(C, j):
            def f(e):
                i = C * G + j
                W(e, "r2w", 16 * (i + 1))
                W(e, "outw", 16 * (i - 1))
                dma(e, rrt[:, j % 2, :],
                    r2s[j * 128:(j + 1) * 128, C * TCB:(C + 1) * TCB],
                    f"r2r{i % 2}", 128 * TCB * 4)
            return f

        def b_apply2(C, j):
            i = C * G + j

            def f(e):
                W(e, "kevac", self.ln_kevac[2 * C + 1])
                W(e, f"r2r{i % 2}", 16 * (i // 2 + 1))
                e.tensor_tensor(out=rrt[:, j % 2, :], in0=rrt[:, j % 2, :],
                                in1=mbt[:], op=SUB)
                cost(_c_dve(TCB))
                inc("outv", e.tensor_tensor(out=rrt[:, j % 2, :],
                                            in0=rrt[:, j % 2, :],
                                            in1=sbt[:], op=MUL))
                cost(_c_dve(TCB))
            return f

        def b_outw(C, j):
            def f(e):
                W(e, "outv", C * G + j + 1)
                dma(e, outT[j * 128:(j + 1) * 128, C * TCB:(C + 1) * TCB],
                    rrt[:, j % 2, :], "outw", 128 * TCB * 4)
            return f

        def fin_gp(e):
            if self.phase == "full":
                W(e, "outw", 16 * G * NCB)
            if debug and self.phase not in ("A0", "P1", "P2"):
                W(e, "ydbg", 16 * self.nca)
            if self.phase == "A0":
                W(e, "dl", G * self.nca)
                W(e, "kevac", 32 * self.nca)
            if self.phase == "P1":
                W(e, "dtdbf", 1)
                W(e, "bcbf", 1)
            if self.phase == "P2":
                W(e, "dl", G)
        w3_end_bps = {}

        opf_bfree_of = {}
        f3_bfree_of = {}

        def mk_op(C, j):
            def make():
                lf, pidx = b_wload("op", j)
                cf, bidx = b_chain(C, "op", j, pidx,
                                   ("y", 16 * (2 * C + 2)) if j == 0 else None)
                ef = b_evac(C, "op", j, bidx)
                opf_bfree_of[(C, j)] = [bidx + 1]
                return [("gp", lf), ("pe", cf), ("scalar", ef)]
            return make

        def mk_stt(C, j):
            def make():
                return [("vector", b_stt(C, j, opf_bfree_of[(C, j)])),
                        ("sync", b_r1w(C, j))]
            return make

        def mk_rbf(C, ln, j):
            def make():
                rf, ridx = b_rbf(C, ln, j)
                return [("scalar", rf),
                        ("pe", b_statmm(2 * C + ln, j, ridx))]
            return make

        def mk_ln1(C):
            def make():
                return [("vector", b_lnm_v(2 * C)),
                        ("scalar", b_lnm_s2(2 * C)),
                        ("pe", b_lnm_pe(2 * C)),
                        ("scalar", b_lnm_s3(2 * C)),
                        ("scalar", b_r1r_prime(C))]
            return make

        def mk_app1(C):
            def make():
                out = []
                for j in range(G):
                    out.append(("vector", b_apply1(C, j)))
                    out.append(("sync", b_xn1w(C, j)))
                    out.append(("scalar",
                                b_xn1bf(C, j, w3_end_bps.get(C - 1))))
                return out
            return make

        def mk_w(C, stage, j):
            def make():
                lf, pinfo = b_wload(stage, j)
                fw = ("xn1bf", G * (C + 1)) if (stage == "w1" and j == 0) \
                    else None
                cf, bidx = b_chain(C, stage, j, pinfo, fw)
                out = [("gp", lf), ("pe", cf),
                       ("scalar", b_evac(C, stage, j, bidx))]
                if stage == "w3":
                    f3_bfree_of[(C, j)] = [bidx + 1]
                if stage == "w3" and j == G - 1:
                    w3_end_bps[C] = self.bch_idx[0]
                return out
            return make

        def mk_xn1rd_prime(C):
            def make():
                return [("scalar", b_xn1r_prime(C))]
            return make

        def mk_r2(C, j):
            def make():
                return [("vector", b_r2(C, j, f3_bfree_of[(C, j)])),
                        ("sync", b_r2w(C, j))]
            return make

        def mk_ln2(C):
            def make():
                return [("vector", b_lnm_v(2 * C + 1)),
                        ("scalar", b_lnm_s2(2 * C + 1)),
                        ("pe", b_lnm_pe(2 * C + 1)),
                        ("scalar", b_lnm_s3(2 * C + 1)),
                        ("scalar", b_r2r_loads(C))]
            return make

        def mk_app2(C):
            def make():
                out = []
                for j in range(G):
                    out.append(("vector", b_apply2(C, j)))
                    out.append(("sync", b_outw(C, j)))
                    if j + 2 < G:
                        out.append(("scalar", b_r2r_ld(C, j + 2)))
                return out
            return make

        def mk_xb(C):
            def make():
                return [("scalar", b_xb_prime(C))]
            return make

        for C in (range(NCB) if self.phase == "full" else []):
            P0 = 16 * (2 * C + 2)
            at(P0 + 0, None, mk_xb(C))
            for j in range(JB):
                at(P0 + 1 + j // 2, None, mk_op(C, j))
                at(P0 + 2 + j // 2, None, mk_stt(C, j))
                at(P0 + 3 + j // 2, None, mk_rbf(C, 0, j))
            at(P0 + 12, None, mk_ln1(C))
            at(P0 + 13, None, mk_app1(C))
            for j in range(JF):
                at(P0 + 14 + j // 8, None, mk_w(C, "w1", j))
            for j in range(JF):
                at(P0 + 18 + j // 8, None, mk_w(C, "w2", j))
            at(P0 + 22, None, mk_xn1rd_prime(C))
            for j in range(G):
                at(P0 + 22 + j // 2, None, mk_w(C, "w3", j))
                at(P0 + 23 + j // 2, None, mk_r2(C, j))
                at(P0 + 24 + j // 2, None, mk_rbf(C, 1, j))
            at(P0 + 31, None, mk_ln2(C))
            at(P0 + 31, None, mk_app2(C))

        def emit_pos(P):
            for (eng, make) in self.bpos.pop(P, []):
                for (eng2, fn) in make():
                    E_(eng2, fn)

        for g in range(G):
            E_("gp", a_xload(0, g))
        E_("pe", a_mm(0))
        E_("scalar", a_evac(0))
        if self.phase not in ("P1",):
            for g in range(G):
                E_("pe", a_dtTg(0, g))
                E_("scalar", a_spg(0, g))
                E_("vector", a_dladd(0, g))
        if self.phase not in ("P1", "P2"):
            emit_bcflat(0)

        NCAr = self.nca
        for c in range(NCAr):
            for g in range(G):
                P = 16 * c + g
                if self.phase not in ("A0", "P1", "P2"):
                    E_("vector", a_du(c, g))
                    for q in range(NDUO):
                        E_("scalar", a_E(c, g, q))
                        E_("vector", a_scanq(c, g, q))
                    E_("vector", a_tree(c, g))
                if (self.phase not in ("A0", "P1", "P2") and c * G + g >= 1
                        and c * G + g + 2 < NCAr * G):
                    E_("gp", a_xduload(c, g))
                if self.phase not in ("A0", "P1", "P2") and c * G + g >= 8:
                    c2, g2 = divmod(c * G + g - 8, G)
                    if c2 <= NCAr - 2:
                        E_("scalar", a_hdma(c2, g2))
                emit_pos(P)
                if c + 1 < NCAr and g == 0:
                    for g2 in range(G):
                        E_("gp", a_xload(c + 1, g2))
                    E_("pe", a_mm(c + 1))
                    E_("scalar", a_evac(c + 1))
                if c + 1 < NCAr:
                    E_("pe", a_dtTg(c + 1, g))
                    E_("scalar", a_spg(c + 1, g))
                    E_("vector", a_dladd(c + 1, g))
            if debug and self.phase not in ("A0", "P1", "P2"):
                E_("gp", a_ydbg(c))
            if c + 1 < NCAr:
                emit_bcflat(c + 1)
        for key in sorted(self.bpos.keys()):
            for (eng, make) in self.bpos[key]:
                for (eng2, fn) in make():
                    E_(eng2, fn)
        self.bpos = {}
        if self.snap:
            def snap_dump(e):
                W(e, "scan", NCAr * G * NDUO)
                dma(e, r1s[0:1024, :].rearrange("(a p) m -> p a m", p=128),
                    h1[:].rearrange("p (x y) b -> p x (y b)", y=4), "ydbg",
                    128 * 16384 * 2)
                dma(e, r1s[1024:1152, 0:256], Hcar[:].rearrange("p n d -> p (n d)"), "ydbg", 128 * 256 * 2)
                dma(e, r1s[1152:1280, 0:256], HcarD[:].rearrange("p n d -> p (n d)"), "ydbg", 128 * 256 * 2)
            E_("gp", snap_dump)
        E_("gp", fin_gp)

        class _MockInstr:
            def then_inc(self, sem, n):
                return self

        class _MockEng:
            def __getattr__(self, name):
                def fn(*a, **k):
                    return _MockInstr()
                return fn

        mock = _MockEng()
        self.mode = "plan"
        for eng, fn in self.ev:
            self.cur = eng
            fn(mock)
        self.CNT = {n: 0 for n in self.CNT}
        self.kp_idx[0] = 0
        self.ke_idx[0] = 0
        self.mode = "emit"

        blk = es.enter_context(nc.Block())
        b = self

        @blk.sync
        def _(e):
            b.cur = "sync"
            for eng, fn in b.ev:
                if eng == "sync":
                    fn(e)

        @blk.scalar
        def _(e):
            b.cur = "scalar"
            for eng, fn in b.ev:
                if eng == "scalar":
                    fn(e)

        @blk.vector
        def _(e):
            b.cur = "vector"
            for eng, fn in b.ev:
                if eng == "vector":
                    fn(e)

        @blk.tensor
        def _(e):
            b.cur = "pe"
            for eng, fn in b.ev:
                if eng == "pe":
                    fn(e)

        @blk.gpsimd
        def _(e):
            b.cur = "gp"
            for eng, fn in b.ev:
                if eng == "gp":
                    fn(e)

        es.close()
        return nc

    def check_schedule(self, dma_GBps=300.0, verbose=True):
        sems = {n: 0 for n in self.CNT}
        streams = {k: list(v) for k, v in self.shadow.items()}
        ptr = {k: 0 for k in streams}
        clock = {k: 0.0 for k in streams}
        pending = []
        busy = {k: 0.0 for k in streams}
        import heapq
        while True:
            progressed = False
            for eng in streams:
                ops = streams[eng]
                while ptr[eng] < len(ops):
                    op = ops[ptr[eng]]
                    if op[0] == "w":
                        _, sem, val = op
                        if sems[sem] >= val:
                            ptr[eng] += 1
                            progressed = True
                            continue
                        due = [p for p in pending if p[1] == sem]
                        due.sort()
                        sat = False
                        cnt = sems[sem]
                        for (t, s, n) in due:
                            cnt += n
                            if cnt >= val:
                                tt = max(t, clock[eng])
                                clock[eng] = tt
                                for p in list(pending):
                                    if p[0] <= t:
                                        sems[p[1]] += p[2]
                                        pending.remove(p)
                                sat = sems[sem] >= val
                                break
                        if sat:
                            ptr[eng] += 1
                            progressed = True
                            continue
                        break
                    elif op[0] == "i":
                        _, sem, tot, n = op
                        sems[sem] += n
                        ptr[eng] += 1
                        progressed = True
                    elif op[0] == "c":
                        clock[eng] += op[1]
                        busy[eng] += op[1]
                        ptr[eng] += 1
                        progressed = True
                    elif op[0] == "d":
                        _, sem, tot, nbytes = op
                        clock[eng] += 500
                        t_done = clock[eng] + 1000 + nbytes / dma_GBps
                        pending.append((t_done, sem, 16))
                        ptr[eng] += 1
                        progressed = True
            if all(ptr[k] >= len(streams[k]) for k in streams):
                break
            if not progressed:
                if pending:
                    t0 = min(p[0] for p in pending)
                    for p in list(pending):
                        if p[0] <= t0:
                            sems[p[1]] += p[2]
                            pending.remove(p)
                    for eng in clock:
                        if ptr[eng] < len(streams[eng]):
                            clock[eng] = max(clock[eng], t0)
                    continue
                info = {}
                for eng in streams:
                    if ptr[eng] < len(streams[eng]):
                        info[eng] = streams[eng][ptr[eng]]
                raise RuntimeError(f"DEADLOCK: stalled at {info}; "
                                   f"sems={ {k: v for k, v in sems.items()} }")
        if verbose:
            tot = max(clock.values())
            print(f"sim total = {tot/1e3:.0f} us; busy: " +
                   ", ".join(f"{k}={busy[k]/1e3:.0f}" for k in busy))
        for eng, ops in self.shadow.items():
            for op in ops:
                if op[0] == "w":
                    assert self.CNT[op[1]] >= op[2], \
                        f"unreachable wait {op} on {eng}"
        return max(clock.values())


def host_inputs(inp, core):
    x = np.ascontiguousarray(np.asarray(inp["L0_attn_hidden_states"])[core].T)
    xpw = np.asarray(inp["L0_attn_x_proj_weight"])
    dtw = np.asarray(inp["L0_attn_dt_proj_weight"])

    def gmaj(v, width):
        return np.ascontiguousarray(np.asarray(v).reshape(width, 128).T)

    def panels(w):
        w = np.asarray(w)
        return np.ascontiguousarray(w.T.reshape(w.shape[1], -1, 128)
                                    .transpose(1, 0, 2))

    alog = np.asarray(inp["L0_attn_A_log"])
    return {
        "xT": x,
        "xpdt": np.ascontiguousarray(xpw[:128, :].T),
        "xpbc": np.ascontiguousarray(xpw[128:160, :].T),
        "dtwT": np.ascontiguousarray(dtw.T),
        "dtb_g": gmaj(inp["L0_attn_dt_proj_bias"], G),
        "alog_g": np.ascontiguousarray(
            alog.reshape(G, 128, NST).transpose(1, 0, 2).reshape(128,
                                                                 G * NST)),
        "dd_g": gmaj(inp["L0_attn_D"], G),
        "opb_g": gmaj(inp["L0_attn_out_proj_bias"], G),
        "b1_g": gmaj(inp["L0_ffn_b1"], JF),
        "b2_g": gmaj(inp["L0_ffn_b2"], JF),
        "b3_g": gmaj(inp["L0_ffn_b3"], G),
        "wop": panels(inp["L0_attn_out_proj_weight"]),
        "w1p": panels(inp["L0_ffn_W1"]),
        "w2p": panels(inp["L0_ffn_W2"]),
        "w3p": panels(inp["L0_ffn_W3"]),
    }


if __name__ == "__main__":
    b = Builder(debug=True)
    nc = b.build()
    t = b.check_schedule()
    ni = sum(len(v) for v in b.shadow.values())
    print(f"build ok; shadow items={ni}")


from concourse.bass_utils import run_bass_kernel_spmd as _run_spmd


def kernel(**inputs):
    b = Builder(debug=False, phase="full")
    nc = b.build()
    in_maps = [host_inputs(inputs, core) for core in range(8)]
    res = _run_spmd(nc, in_maps, core_ids=list(range(8)))
    out = np.stack([np.asarray(res.results[c]["outT"]).T
                    for c in range(8)], 0)
    return np.ascontiguousarray(out, dtype=np.float32)


# version 02
# speedup vs baseline: 1.0112x; 1.0112x over previous
import numpy as np
from contextlib import ExitStack
import concourse.bass as bass
import concourse.mybir as mybir
from concourse.alu_op_type import AluOpType

F32 = mybir.dt.float32
BF16 = mybir.dt.bfloat16
AF = mybir.ActivationFunctionType
MUL = AluOpType.mult
ADD = AluOpType.add
SUB = AluOpType.subtract

DM, SEQ, NST, DTR, DFF = 2048, 2048, 16, 128, 4096
G = DM // 128
TCA, TCB = 256, 512
NCA, NCB = SEQ // TCA, SEQ // TCB
QW, NDUO, DW = TCA + 2, 8, 2
JB, JF = DM // 128, DFF // 128
LNEPS = 1e-5
INV_D = 1.0 / DM

def _c_dve(fd, rate=1.074):
    return 100 + fd * rate


class Builder:
    def __init__(self, debug=False, phase="full", nca=NCA, scandrain=False,
                 snap=False):
        self.debug = debug
        self.phase = phase
        self.nca = nca
        self.scandrain = scandrain
        self.snap = snap
        self.bp0 = 33
        self.bstep = 50
        self.wdens = (2, 2, 1)
        self.nc = bass.Bass("TRN2", target_bir_lowering=False)
        self.es = ExitStack()
        self.dram = {}
        self.CNT = {}
        self.S = {}
        self.shadow = {k: [] for k in ["sync", "scalar", "vector", "pe", "gp"]}
        self.cur = None
        self.ev = []
        self.wp_idx = [0]
        self.bch_idx = [0]
        self.st_idx = [0]
        self.kp_idx = [0]
        self.ke_idx = [0]
        self.bpos = {}
        self.dpend = {}
        self.mode = "plan"
        self.planned = {k: [] for k in self.shadow}
        self.pptr = {k: 0 for k in self.shadow}
        self.piece_chain = {}
        self.op_end_bfree = {}
        self.w3_end_bps = {}
        self.ln_kevac = {}
        self.bbc_kps = {}

    DMASEMS = {"c", "cg", "r1w", "xn1w", "r2w", "outw", "ydbg", "hd"}

    def W(self, e, name, val):
        if self.mode == "plan":
            if name in self.DMASEMS:
                assert self.CNT[name] >= val, (name, val, self.CNT[name])
                val = self.CNT[name]
            self.planned[self.cur].append(val)
        else:
            val = self.planned[self.cur][self.pptr[self.cur]]
            self.pptr[self.cur] += 1
        if val > 0:
            e.wait_ge(self.S[name], val)
            if self.mode != "plan":
                self.shadow[self.cur].append(("w", name, val))

    def inc(self, name, instr, n=1):
        instr.then_inc(self.S[name], n)
        self.CNT[name] += n
        if self.mode != "plan":
            self.shadow[self.cur].append(("i", name, self.CNT[name], n))

    def cost(self, ns):
        if self.mode != "plan":
            self.shadow[self.cur].append(("c", ns))

    def dma(self, e, out, in_, sem, nbytes):
        instr = e.dma_start(out=out, in_=in_)
        instr.then_inc(self.S[sem], 16)
        self.CNT[sem] += 16
        if self.mode != "plan":
            self.shadow[self.cur].append(("d", sem, self.CNT[sem], nbytes))
        return instr

    def E_(self, eng, fn):
        self.ev.append((eng, fn))

    def at(self, pos, eng, fn):
        self.bpos.setdefault(pos, []).append((eng, fn))

    def build(self):
        nc, es = self.nc, self.es
        debug = self.debug

        def din(name, shape, dt=F32):
            self.dram[name] = nc.dram_tensor(name, shape, dt,
                                             kind="ExternalInput").ap()
            return self.dram[name]

        xT = din("xT", [DM, SEQ])
        xpdt = din("xpdt", [DM, DTR])
        xpbc = din("xpbc", [DM, 2 * NST])
        dtwT = din("dtwT", [DTR, DM])
        for nm, sh in [("dtb_g", [128, G]), ("alog_g", [128, G * NST]),
                       ("dd_g", [128, G]), ("opb_g", [128, G]),
                       ("b1_g", [128, JF]), ("b2_g", [128, JF]),
                       ("b3_g", [128, G])]:
            din(nm, sh)
        wsrc = {"op": din("wop", [JB, 128, DM]),
                "w1": din("w1p", [JF, 128, DM]),
                "w2": din("w2p", [JF, 128, DFF]),
                "w3": din("w3p", [G, 128, DFF])}
        KTs = {"op": G, "w1": G, "w2": JF, "w3": JF}

        outT = nc.dram_tensor("outT", [DM, SEQ], F32, kind="ExternalOutput").ap()
        self.outT = outT
        skind = "ExternalOutput" if debug else "Internal"
        r1s = nc.dram_tensor("r1s", [DM, SEQ], F32, kind=skind).ap()
        xn1s = nc.dram_tensor("xn1s", [DM, SEQ], F32, kind=skind).ap()
        r2s = nc.dram_tensor("r2s", [DM, SEQ], F32, kind=skind).ap()
        if debug:
            y_dbg = nc.dram_tensor("y_dbg", [DM, SEQ], BF16,
                                   kind="ExternalOutput").ap()

        def sbuf(name, shape, dt):
            return es.enter_context(nc.sbuf_tensor(name, shape, dt))

        xpdt_bf = sbuf("xpdt_bf", [128, G, 128], BF16)
        xpbc_bf = sbuf("xpbc_bf", [128, G, 32], BF16)
        dtwT_bf = sbuf("dtwT_bf", [128, DM], BF16)
        dtb_sb = sbuf("dtb_sb", [128, G], F32)
        A_sb = sbuf("A_sb", [128, G * NST], F32)
        oneD = sbuf("oneD", [128, G], F32)
        opb_sb = sbuf("opb_sb", [128, G], F32)
        b1_sb = sbuf("b1_sb", [128, JF], F32)
        b2_sb = sbuf("b2_sb", [128, JF], F32)
        b3_sb = sbuf("b3_sb", [128, G], F32)
        ones1 = sbuf("ones1", [1, 128], BF16)
        ones_red = sbuf("ones_red", [128, 1], BF16)
        cst = sbuf("cst", [128, 3], F32)
        spt = sbuf("spt", [128, 2, 2, TCA], F32)
        cst1 = sbuf("cst1", [1, 2], F32)
        xmm = sbuf("xmm", [128, 3, TCA], BF16)
        xdu = sbuf("xdu", [128, 3, TCA], BF16)
        dtd_bf = sbuf("dtd_bf", [128, 2, TCA], BF16)
        bc_bf = sbuf("bc_bf", [32, 2, TCA], BF16)
        bc_row = sbuf("bc_row", [1, 2, 4, TCA], BF16)
        Bb = sbuf("Bb", [128, NST, TCA], BF16)
        Cb = sbuf("Cb", [128, NST, TCA], BF16)
        delta = sbuf("delta", [128, 4, 8, TCA], BF16)
        Et = sbuf("Et", [128, 2, DW * QW], F32)
        Ut = sbuf("Ut", [128, DW * QW], BF16)
        Ht = sbuf("Ht", [128, DW * QW], BF16)
        tmp = sbuf("tmp", [128, NST, TCA], BF16)
        du_t = sbuf("du_t", [128, TCA], BF16)
        Hcar = sbuf("Hcar", [128, DW, G * NDUO], BF16)
        HcarD = sbuf("HcarD", [128, DW, G * NDUO], BF16)
        yb = sbuf("yb", [128, 3, G, TCA], BF16)
        rbf2 = sbuf("rbf2", [128, 1, 2, TCB], BF16)
        stw = sbuf("stw", [1, 2, TCB], F32)
        stbf = sbuf("stbf", [1, 2, TCB], BF16)
        mbt = sbuf("mbt", [128, TCB], BF16)
        sbt = sbuf("sbt", [128, TCB], BF16)
        r1t = sbuf("r1t", [128, 3, TCB], F32)
        opf = sbuf("opf", [128, 2, TCB], F32)
        xBt = sbuf("xBt", [128, 2, TCB], F32)
        xn1bf = sbuf("xn1bf", [128, G, TCB], BF16)
        h1 = sbuf("h1", [128, JF, TCB], BF16)
        h2b = sbuf("h2b", [128, G, TCB], BF16)
        f3t = sbuf("f3t", [128, 2, TCB], F32)
        xn1r = sbuf("xn1r", [128, 2, TCB], F32)
        rrt = sbuf("rrt", [128, 2, TCB], F32)
        wsl = sbuf("wsl", [128, 3, G, 128], BF16)

        ps_a = es.enter_context(nc.psum_tensor("ps_a", [128, 2 * TCA], F32))
        ps_bc = es.enter_context(nc.psum_tensor("ps_bc", [32, TCA], F32))
        ps_st = es.enter_context(nc.psum_tensor("ps_st", [1, 2 * TCB], F32))
        ps_k = [es.enter_context(nc.psum_tensor(f"ps_k{i}", [128, TCB], F32))
                for i in range(2)]
        ps_b = [es.enter_context(nc.psum_tensor(f"ps_b{i}", [128, TCB], F32))
                for i in range(2)]
        ps_dtdA = ps_a[:, 0:TCA]
        ps_dtTA = [ps_a[:, 0:TCA], ps_a[:, TCA:2 * TCA]]

        semnames = (
            ["c", "cg", "init", "xfree", "du", "dtdps", "bcps",
             "dtdbf", "bcbf", "dtT", "sp", "spc", "dl", "kps",
             "kevac", "bbfree", "E", "scan", "y", "Bps",
             "Bfree", "r1", "r1w", "rbf", "stmm", "stps", "lnm",
             "rstd", "xn1f", "xn1w", "xn1bf", "r2", "r2w",
             "outv", "outw", "ydbg", "hd"]
            + [f"xmm{i}" for i in range(3)] + [f"xdu{i}" for i in range(3)]
            + [f"wp{i}" for i in range(3)] + [f"xB{i}" for i in range(2)]
            + [f"r1r{i}" for i in range(2)] + [f"xn1rd{i}" for i in range(2)]
            + [f"r2r{i}" for i in range(2)] + [f"bcrow{i}" for i in range(2)]
        )
        self.S = {n: es.enter_context(nc.semaphore("s_" + n))
                  for n in semnames}
        self.CNT = {n: 0 for n in semnames}
        W, inc, dma, cost = self.W, self.inc, self.dma, self.cost
        S, CNT = self.S, self.CNT
        E_, at = self.E_, self.at

        def st_sync(e):
            for nm, dst in [("dtb_g", dtb_sb), ("alog_g", A_sb),
                            ("dd_g", oneD), ("opb_g", opb_sb),
                            ("b1_g", b1_sb), ("b2_g", b2_sb),
                            ("b3_g", b3_sb)]:
                dma(e, dst[:], self.dram[nm], "c", 128 * 64 * 4)
        E_("sync", st_sync)

        def st_gp(e):
            dma(e, xpdt_bf[:], xpdt.rearrange("(g p) m -> p g m", p=128),
                "cg", DM * DTR * 4)
            dma(e, xpbc_bf[:], xpbc.rearrange("(g p) m -> p g m", p=128),
                "cg", DM * 32 * 4)
            dma(e, dtwT_bf[:], dtwT, "cg", DTR * DM * 4)
        E_("gp", st_gp)

        def st_vec0(e):
            e.memset(ones1[:], 1.0)
            e.memset(ones_red[:], 1.0)
            e.memset(Hcar[:], 0.0)
            e.memset(cst[:, 0:1], 1.0)
            e.memset(cst[:, 1:2], LNEPS)
            e.memset(cst[:, 2:3], -1.0)
            e.memset(cst1[:, 0:1], LNEPS)
            e.memset(cst1[:, 1:2], -0.5)
            u3 = Ut[:].rearrange("p (n w) -> p n w", n=DW)
            e.memset(u3[:, :, 0:2], 0.0)
            for s in range(2):
                e3 = Et[:, s, :].rearrange("p (n w) -> p n w", n=DW)
                m = e.memset(e3[:, :, 0:2], 0.0)
            inc("init", m)
            cost(2000)
        E_("vector", st_vec0)

        def st_gp0(e):
            for g in range(3):
                dma(e, xdu[:, g, :], xT[g * 128:(g + 1) * 128, 0:TCA],
                    f"xdu{g}", 128 * TCA * 4)
        E_("gp", st_gp0)

        def st_scalar(e):
            W(e, "c", 112)
            W(e, "cg", 48)
            W(e, "init", 1)
            inc("init", e.activation(out=A_sb[:], in_=A_sb[:], func=AF.Exp))
            cost(1000)
        E_("scalar", st_scalar)

        def st_vec(e):
            W(e, "init", 2)
            e.tensor_scalar_mul(out=A_sb[:], in0=A_sb[:], scalar1=-1.0)
            W(e, "c", 112)
            inc("init", e.tensor_scalar(out=oneD[:], in0=oneD[:], scalar1=1.0,
                                        scalar2=None, op0=ADD))
            cost(500)
        E_("vector", st_vec)

        def a_xload(c, g):
            i = c * G + g

            def f(e):
                if i >= 3:
                    ct, gt = divmod(i - 3, G)
                    if gt >= 14:
                        W(e, "bcps", ct + 1)
                    else:
                        W(e, "xfree", 7 * ct + gt // 2 + 1)
                dma(e, xmm[:, i % 3, :],
                    xT[g * 128:(g + 1) * 128, c * TCA:(c + 1) * TCA],
                    f"xmm{i % 3}", 128 * TCA * 4)
            return f

        def a_mm(c):
            def f(e):
                if c == 0:
                    W(e, "c", 112)
                    W(e, "cg", 48)
                    W(e, "init", 1)
                if c >= 1:
                    W(e, "sp", G * c - 1)
                if c >= 2:
                    W(e, "dtT", G * (c - 1))
                    W(e, "bcbf", 0)
                for g in range(G):
                    i = c * G + g
                    W(e, f"xmm{i % 3}", 16 * (i // 3 + 1))
                    md = e.matmul(out=ps_dtdA, lhsT=xpdt_bf[:, g, :],
                                  rhs=xmm[:, i % 3, :], start=(g == 0),
                                  stop=(g == G - 1))
                    m = e.matmul(out=ps_bc[:], lhsT=xpbc_bf[:, g, :],
                                 rhs=xmm[:, i % 3, :], start=(g == 0),
                                 stop=(g == G - 1))
                    if g == G - 1:
                        inc("dtdps", md)
                        inc("bcps", m)
                    elif g % 2 == 1:
                        inc("xfree", m)
                    cost(2 * TCA * 0.42 + 60)
            return f

        def a_evac(c):
            def f(e):
                W(e, "dtdps", c + 1)
                if c >= 2:
                    W(e, "dtT", G * (c - 1))
                inc("dtdbf", e.activation(out=dtd_bf[:, c % 2, :],
                                          in_=ps_dtdA, func=AF.Copy))
                cost(600)
                W(e, "bcps", c + 1)
                if c >= 2:
                    Bp = 8 * (c - 2) + 7
                    W(e, f"bcrow{Bp % 2}", 16 * (Bp // 2 + 1))
                inc("bcbf", e.activation(out=bc_bf[:, c % 2, :],
                                         in_=ps_bc[:], func=AF.Copy))
                cost(600)
            return f

        def a_dtTg(c, g):
            def f(e):
                if g == 0:
                    W(e, "dtdbf", c + 1)
                if g >= 2:
                    W(e, "sp", G * c + g - 1)
                inc("dtT", e.matmul(
                    out=ps_dtTA[g % 2],
                    lhsT=dtwT_bf[:, g * 128:(g + 1) * 128],
                    rhs=dtd_bf[:, c % 2, :], start=True, stop=True))
                cost(TCA * 0.42 + 120)
            return f

        def a_spg(c, g):
            i = c * G + g

            def f(e):
                W(e, "dtT", G * c + g + 1)
                if i >= 2:
                    W(e, "dl", i - 1)
                e.activation(out=spt[:, i % 2, 1, :], in_=ps_dtTA[g % 2],
                             func=AF.Relu, bias=dtb_sb[:, g:g + 1])
                cost(520)
                inc("sp", e.activation(out=spt[:, i % 2, 0, :],
                                       in_=ps_dtTA[g % 2],
                                       func=AF.Abs, bias=dtb_sb[:, g:g + 1]))
                e.drain()
                cost(700)
                e.activation(out=spt[:, i % 2, 0, :],
                             in_=spt[:, i % 2, 0, :], func=AF.Exp,
                             scale=cst[:, 2:3])
                e.drain()
                cost(700)
                a = e.activation(out=spt[:, i % 2, 0, :],
                                 in_=spt[:, i % 2, 0, :], func=AF.Ln,
                                 bias=cst[:, 0:1])
                cost(520)
                inc("spc", a)
            return f

        def a_dladd(c, g):
            i = c * G + g
            H = 2 * c + g // 8

            def f(e):
                W(e, "spc", i + 1)
                inc("dl", e.tensor_tensor(out=delta[:, H % 4, g % 8, :],
                                          in0=spt[:, i % 2, 0, :],
                                          in1=spt[:, i % 2, 1, :], op=ADD))
                cost(_c_dve(TCA))
            return f

        def a_bcflat(c, h):
            B = 8 * c + h

            def fs(e):
                if B >= 2:
                    W(e, "kps", self.bbc_kps[B - 2])
                dma(e, bc_row[0:1, h % 2, :, :],
                    bc_bf[h * 4:(h + 1) * 4, c % 2, :],
                    f"bcrow{B % 2}", 4 * TCA * 2)

            def fp(e):
                if h == 0:
                    W(e, "bcbf", c + 1)
                    if c == 0:
                        W(e, "init", 1)
                    if c >= 1:
                        W(e, "bbfree", c)
                W(e, f"bcrow{B % 2}", 16 * (B // 2 + 1))
                for half in range(2):
                    ki = self.kp_idx[0]
                    self.kp_idx[0] += 1
                    if ki >= 2:
                        W(e, "kevac", ki - 1)
                    for cc in range(2):
                        col = h * 4 + half * 2 + cc
                        m = e.matmul(
                            out=ps_k[ki % 2][:, cc * TCA:(cc + 1) * TCA],
                            lhsT=ones1[:],
                            rhs=bc_row[0:1, h % 2, col % 4, :],
                            start=True, stop=True)
                        cost(TCA * 0.42 + 60)
                    inc("kps", m)
                self.bbc_kps[B] = self.CNT["kps"]

            def fe(e):
                for half in range(2):
                    col = h * 4 + half * 2
                    ki = self.ke_idx[0]
                    self.ke_idx[0] += 1
                    W(e, "kps", ki + 1)
                    dst = Bb if col < NST else Cb
                    inc("kevac", e.activation(
                        out=dst[:, col % NST:col % NST + 2, :],
                        in_=ps_k[ki % 2][:], func=AF.Copy))
                    cost(940)
            return fs, fp, fe

        def emit_bcflat(c):
            for h in range(8):
                fs, fp, fe = a_bcflat(c, h)
                E_("scalar", fs)
                E_("pe", fp)
                E_("scalar", fe)

        def a_du(c, g):
            i = c * G + g

            def f(e):
                W(e, "sp", G * c + g + 1)
                W(e, f"xdu{i % 3}", 16 * (i // 3 + 1))
                if g == 0:
                    W(e, "kevac", 16 * (c + 1))
                e.tensor_tensor(out=du_t[:],
                                in0=delta[:, (2 * c + g // 8) % 4, g % 8, :],
                                in1=xdu[:, i % 3, :], op=MUL)
                cost(_c_dve(TCA, 0.567))
            return f

        def a_E(c, g, q):
            eidx = (c * G + g) * NDUO + q

            def f(e):
                if eidx == 0:
                    W(e, "init", 3)
                if q == 0:
                    W(e, "dl", c * G + g + 1)
                if eidx >= 2:
                    W(e, "scan", eidx - 1)
                for nn in range(DW):
                    n = q * DW + nn
                    a = e.activation(
                        out=Et[:, eidx % 2, nn * QW + 2:nn * QW + 2 + TCA],
                        in_=delta[:, (2 * c + g // 8) % 4, g % 8, :], func=AF.Exp,
                        scale=A_sb[:, g * NST + n:g * NST + n + 1])
                    cost(520)
                    if nn == DW - 1:
                        inc("E", a)
            return f

        def a_xduload(c, g):
            i2 = c * G + g + 2

            def f(e):
                c2, g2 = divmod(i2, G)
                W(e, "du", i2 - 2)
                dma(e, xdu[:, i2 % 3, :],
                    xT[g2 * 128:(g2 + 1) * 128, c2 * TCA:(c2 + 1) * TCA],
                    f"xdu{i2 % 3}", 128 * TCA * 4)
            return f

        u3v = Ut[:].rearrange("p (n w) -> p n w", n=DW)
        h3v = Ht[:].rearrange("p (n w) -> p n w", n=DW)

        def a_scanq(c, g, q):
            eidx = (c * G + g) * NDUO + q

            def f(e):
                if c >= 1 and q == 0:
                    W(e, "hd", 16)
                gd = g * NDUO + q
                csrc = (Hcar if c == 0 else HcarD)[:, :, gd:gd + 1]
                e.tensor_copy(out=u3v[:, :, 1:2], in_=csrc)
                cost(160)
                for nn in range(DW):
                    n = q * DW + nn
                    e.tensor_tensor(out=u3v[:, nn, 2:QW], in0=du_t[:],
                                    in1=Bb[:, n, :], op=MUL)
                    cost(_c_dve(TCA, 0.567))
                if eidx >= 1:
                    pg, pq = divmod((eidx - 1) % (G * NDUO), NDUO)
                    e.tensor_copy(out=Hcar[:, :, pg * NDUO + pq:
                                           pg * NDUO + pq + 1],
                                  in_=h3v[:, :, QW - 1:QW])
                    cost(160)
                W(e, "E", eidx + 1)
                if self.scandrain:
                    e.drain()
                    cost(100)
                inc("scan", e.tensor_tensor_scan(
                    out=Ht[:], data0=Et[:, eidx % 2, :], data1=Ut[:],
                    initial=0.0, op0=MUL, op1=ADD))
                cost(330 + DW * QW * 2.13)
                if self.snap and c == 1 and g == 0:
                    hf = h1[:].rearrange("p a b -> p (a b)")
                    e.tensor_copy(out=hf[:, q * 1548:q * 1548 + 516],
                                  in_=Ht[:])
                    e.tensor_copy(out=hf[:, q * 1548 + 516:q * 1548 + 1032],
                                  in_=Ut[:])
                    e.tensor_copy(out=hf[:, q * 1548 + 1032:q * 1548 + 1548],
                                  in_=Et[:, eidx % 2, :])
                    cost(900)

                e.tensor_tensor(out=tmp[:, q * DW:(q + 1) * DW, :],
                                in0=h3v[:, :, 2:QW],
                                in1=Cb[:, q * DW:(q + 1) * DW, :], op=MUL)
                cost(_c_dve(DW * TCA, 0.567))
            return f

        t4 = tmp[:].rearrange("p (a b) t -> p a b t", b=2)

        def a_tree(c, g):
            def f(e):
                if self.phase == "full" and c >= 3 and g == 0:
                    W(e, "Bfree", self.op_end_bfree[(c - 3) // 2])
                inc("du", e.tensor_tensor(out=tmp[:, 0:8, :],
                                          in0=t4[:, :, 0, :],
                                          in1=t4[:, :, 1, :], op=ADD))
                cost(_c_dve(8 * TCA, 0.567))
                e.tensor_tensor(out=tmp[:, 0:4, :], in0=t4[:, 0:4, 0, :],
                                in1=t4[:, 0:4, 1, :], op=ADD)
                cost(_c_dve(4 * TCA, 0.567))
                m3 = e.tensor_tensor(out=tmp[:, 0:2, :], in0=t4[:, 0:2, 0, :],
                                     in1=t4[:, 0:2, 1, :], op=ADD)
                cost(_c_dve(2 * TCA, 0.567))
                if g == G - 1:
                    inc("bbfree", m3)
                inc("y", e.tensor_tensor(out=yb[:, c % 3, g, :],
                                         in0=tmp[:, 0, :], in1=tmp[:, 1, :],
                                         op=ADD))
                cost(_c_dve(TCA, 0.567))
            return f

        def a_hdma(c2, g2):
            def f(e):
                W(e, "scan", 128 * c2 + 8 * (g2 + 2))
                dma(e, HcarD[:, :, g2 * NDUO:(g2 + 1) * NDUO],
                    Hcar[:, :, g2 * NDUO:(g2 + 1) * NDUO], "hd",
                    128 * NST * 2)
            return f

        def a_ydbg(c):
            def f(e):
                W(e, "y", G * (c + 1))
                dma(e, y_dbg[:, c * TCA:(c + 1) * TCA]
                    .rearrange("(g p) t -> p g t", p=128),
                    yb[:, c % 3, :, :], "ydbg", DM * TCA * 2)
            return f

        def b_wload(stage, j):
            KT = KTs[stage]
            npc = KT // G
            p0 = self.wp_idx[0]
            self.wp_idx[0] += npc

            def f(e):
                src3 = wsrc[stage][j].rearrange("p (h m) -> p h m", m=128)
                for pc in range(npc):
                    pidx = p0 + pc
                    if pidx >= 3:
                        W(e, "Bfree", self.piece_chain[pidx - 3])
                    dma(e, wsl[:, pidx % 3, :, :],
                        src3[:, pc * G:(pc + 1) * G, :],
                        f"wp{pidx % 3}", G * 128 * 128 * 4)
            return f, (p0, npc)

        def b_chain(C, stage, j, pinfo, first_wait=None):
            bidx = self.bch_idx[0]
            self.bch_idx[0] += 1
            KT = KTs[stage]
            p0, npc = pinfo

            def f(e):
                if first_wait is not None:
                    W(e, *first_wait)
                for pc in range(npc):
                    W(e, f"wp{(p0 + pc) % 3}", 16 * ((p0 + pc) // 3 + 1))
                if bidx >= 2:
                    W(e, "Bfree", bidx - 1)
                if stage == "op":
                    for half in range(2):
                        for k in range(KT):
                            lhsT = wsl[:, (p0 + k // G) % 3, k % G, :]
                            m = e.matmul(
                                out=ps_b[bidx % 2][:, half * TCA:
                                                   (half + 1) * TCA],
                                lhsT=lhsT,
                                rhs=yb[:, (2 * C + half) % 3, k, :],
                                start=(k == 0), stop=(k == KT - 1))
                            cost(TCA * 0.42 + 120)
                for k in (range(0) if stage == "op" else range(KT)):
                    lhsT = wsl[:, (p0 + k // G) % 3, k % G, :]
                    if False:
                        pass
                    else:
                        rhs = {"w1": lambda: xn1bf[:, k, :],
                               "w2": lambda: h1[:, k, :],
                               "w3": lambda: (xn1bf[:, k, :] if k < G
                                              else h2b[:, k - G, :])}[stage]()
                        m = e.matmul(out=ps_b[bidx % 2][:], lhsT=lhsT, rhs=rhs,
                                     start=(k == 0), stop=(k == KT - 1))
                        cost(TCB * 0.42 + 60)
                inc("Bps", m)
                for pc in range(npc):
                    self.piece_chain[p0 + pc] = bidx + 1
            return f, bidx

        def b_evac(C, stage, j, bidx):
            def f(e):
                W(e, "Bps", bidx + 1)
                if stage == "op":
                    if C * G + j >= 2:
                        W(e, "r1", C * G + j - 1)
                    a = e.activation(out=opf[:, j % 2, :],
                                     in_=ps_b[bidx % 2][:],
                                     func=AF.Relu,
                                     bias=opb_sb[:, j:j + 1])
                elif stage == "w1":
                    a = e.activation(out=h1[:, j, :], in_=ps_b[bidx % 2][:],
                                     func=AF.Relu, bias=b1_sb[:, j:j + 1])
                elif stage == "w2":
                    dst = xn1bf[:, j, :] if j < G else h2b[:, j - G, :]
                    a = e.activation(out=dst, in_=ps_b[bidx % 2][:],
                                     func=AF.Relu, bias=b2_sb[:, j:j + 1])
                else:
                    if C * G + j >= 2:
                        W(e, "r2", C * G + j - 1)
                    a = e.activation(out=f3t[:, j % 2, :],
                                     in_=ps_b[bidx % 2][:],
                                     func=AF.Identity,
                                     bias=b3_sb[:, j:j + 1])

                cost(780)
                inc("Bfree", a)
                if stage == "op" and j == JB - 1:
                    self.op_end_bfree[C] = self.CNT["Bfree"]
            return f

        def b_xb_prime(C):
            def f(e):
                for j in range(2):
                    i = C * G + j
                    if i >= 2:
                        W(e, "r1", i - 1)
                    dma(e, xBt[:, i % 2, :],
                        xT[j * 128:(j + 1) * 128, C * TCB:(C + 1) * TCB],
                        f"xB{i % 2}", 128 * TCB * 4)
            return f

        def r1t_consumers(u):
            Cp, jj = divmod(u, 32)
            if jj < G:
                return [("r1w", 16 * (G * Cp + jj + 1)),
                        ("rbf", 32 * Cp + jj + 1)]
            jj -= G
            return [("r2w", 16 * (G * Cp + jj + 1)),
                    ("rbf", 32 * Cp + G + jj + 1)]

        def b_stt(C, j, opf_bfree):
            i = C * G + j
            u = 32 * C + j

            def f(e):
                W(e, "Bfree", opf_bfree[0])
                W(e, f"xB{i % 2}", 16 * (i // 2 + 1))
                if u >= 3:
                    for nm, v in r1t_consumers(u - 3):
                        W(e, nm, v)
                inc("r1", e.scalar_tensor_tensor(
                    out=r1t[:, u % 3, :], in0=xBt[:, i % 2, :],
                    scalar=oneD[:, j:j + 1], in1=opf[:, j % 2, :],
                    op0=MUL, op1=ADD))
                cost(_c_dve(TCB, 1.09))
            return f

        def b_r1w(C, j):
            def f(e):
                W(e, "r1", C * G + j + 1)
                dma(e, r1s[j * 128:(j + 1) * 128, C * TCB:(C + 1) * TCB],
                    r1t[:, (32 * C + j) % 3, :], "r1w", 128 * TCB * 4)
            return f

        def b_rbf(C, ln, j):
            ridx = self.st_idx[0]
            self.st_idx[0] += 1
            u = 32 * C + ln * G + j

            def f(e):
                W(e, "r1" if ln == 0 else "r2", C * G + j + 1)
                if ln == 1 and j + 2 < G:
                    W(e, "xn1w", 16 * (C * G + j + 3))
                    dma(e, xn1r[:, j % 2, :],
                        xn1s[(j + 2) * 128:(j + 3) * 128,
                             C * TCB:(C + 1) * TCB],
                        f"xn1rd{(C * G + j + 2) % 2}", 128 * TCB * 4)
                if ln == 0 and j + 2 < G:
                    dma(e, xBt[:, (C * G + j + 2) % 2, :],
                        xT[(j + 2) * 128:(j + 3) * 128,
                           C * TCB:(C + 1) * TCB],
                        f"xB{(C * G + j + 2) % 2}", 128 * TCB * 4)
                if ridx >= 1:
                    full, rem = divmod(ridx, G)
                    W(e, "stmm", full * (G - 1) + max(0, rem - 1)
                      + (1 if rem else 0))
                    if rem == 0 and full >= 1:
                        W(e, "stps", full)
                e.activation(out=rbf2[:, 0, 0, :],
                             in_=r1t[:, u % 3, :], func=AF.Copy)
                cost(780)
                inc("rbf", e.activation(out=rbf2[:, 0, 1, :],
                                        in_=r1t[:, u % 3, :], func=AF.Square))
                cost(780)
            return f, ridx

        def b_statmm(L, j, ridx):
            def f(e):
                W(e, "rbf", ridx + 1)
                if j == 0 and L >= 1:
                    W(e, "lnm", L)
                e.matmul(out=ps_st[0:1, 0:TCB], lhsT=ones_red[:],
                         rhs=rbf2[:, 0, 0, :],
                         start=(j == 0), stop=(j == G - 1))
                m = e.matmul(
                    out=ps_st[0:1, TCB:2 * TCB], lhsT=ones_red[:],
                    rhs=rbf2[:, 0, 1, :],
                    start=(j == 0), stop=(j == G - 1))
                cost(2 * TCB * 0.42 + 120)
                if j == G - 1:
                    inc("stps", m)
                else:
                    inc("stmm", m)
            return f

        def b_lnm_v(L):
            def f(e):
                W(e, "stps", L + 1)
                e.tensor_scalar_mul(out=stw[:, 0, :], in0=ps_st[0:1, 0:TCB],
                                    scalar1=INV_D)
                e.tensor_scalar_mul(out=stw[:, 1, :],
                                    in0=ps_st[0:1, TCB:2 * TCB],
                                    scalar1=INV_D)
                e.tensor_tensor(out=ps_st[0:1, 0:TCB], in0=stw[:, 0, :],
                                in1=stw[:, 0, :], op=MUL)
                inc("lnm", e.tensor_tensor(out=stw[:, 1, :], in0=stw[:, 1, :],
                                           in1=ps_st[0:1, 0:TCB], op=SUB))
                cost(4000)
            return f

        def b_lnm_s2(L):
            def f(e):
                W(e, "lnm", L + 1)
                e.activation(out=stw[:, 1, :], in_=stw[:, 1, :], func=AF.Ln,
                             bias=cst1[:, 0:1])
                e.drain()
                e.activation(out=stw[:, 1, :], in_=stw[:, 1, :], func=AF.Exp,
                             scale=cst1[:, 1:2])
                e.drain()
                e.activation(out=stbf[:, 0, :], in_=stw[:, 0, :], func=AF.Copy)
                inc("rstd", e.activation(out=stbf[:, 1, :], in_=stw[:, 1, :],
                                         func=AF.Copy))
                cost(3600)
            return f

        def b_lnm_pe(L):
            def f(e):
                W(e, "rstd", L + 1)
                for i in range(2):
                    ki = self.kp_idx[0]
                    self.kp_idx[0] += 1
                    if ki >= 2:
                        W(e, "kevac", ki - 1)
                    inc("kps", e.matmul(out=ps_k[ki % 2][:],
                                        lhsT=ones1[:],
                                        rhs=stbf[0:1, i, :],
                                        start=True, stop=True))
                    cost(TCB * 0.42 * 4 + 60)
            return f

        def b_lnm_s3(L):
            def f(e):
                if L >= 1:
                    Cp, lnp = divmod(L - 1, 2)
                    if lnp == 0:
                        W(e, "xn1f", G * (Cp + 1))
                    else:
                        W(e, "outv", G * (Cp + 1))
                for i, dst in enumerate([mbt, sbt]):
                    ki = self.ke_idx[0]
                    self.ke_idx[0] += 1
                    W(e, "kps", ki + 1)
                    a = e.activation(out=dst[:], in_=ps_k[ki % 2][:],
                                     func=AF.Copy)
                    inc("kevac", a)
                    cost(780)
                self.ln_kevac[L] = self.CNT["kevac"]
            return f

        def b_r1r_prime(C):
            def f(e):
                if C >= 1:
                    W(e, "outw", 16 * G * C)
                for j in range(2):
                    i = C * G + j
                    W(e, "r1w", 16 * (i + 1))
                    dma(e, rrt[:, j % 2, :],
                        r1s[j * 128:(j + 1) * 128, C * TCB:(C + 1) * TCB],
                        f"r1r{i % 2}", 128 * TCB * 4)
            return f

        def b_apply1(C, j):
            i = C * G + j

            def f(e):
                W(e, "kevac", self.ln_kevac[2 * C])
                W(e, f"r1r{i % 2}", 16 * (i // 2 + 1))
                e.tensor_tensor(out=rrt[:, j % 2, :], in0=rrt[:, j % 2, :],
                                in1=mbt[:], op=SUB)
                cost(_c_dve(TCB))
                inc("xn1f", e.tensor_tensor(out=rrt[:, j % 2, :],
                                            in0=rrt[:, j % 2, :],
                                            in1=sbt[:], op=MUL))
                cost(_c_dve(TCB))
            return f

        def b_xn1w(C, j):
            def f(e):
                W(e, "xn1f", C * G + j + 1)
                dma(e, xn1s[j * 128:(j + 1) * 128, C * TCB:(C + 1) * TCB],
                    rrt[:, j % 2, :], "xn1w", 128 * TCB * 4)
            return f

        def b_xn1bf(C, j, w3_prev_bps):
            def f(e):
                W(e, "xn1f", C * G + j + 1)
                if w3_prev_bps is not None and j == 0:
                    W(e, "Bps", w3_prev_bps)
                inc("xn1bf", e.activation(out=xn1bf[:, j, :],
                                          in_=rrt[:, j % 2, :],
                                          func=AF.Copy))
                cost(780)
                j2 = j + 2
                if j2 < G:
                    i2 = C * G + j2
                    W(e, "r1w", 16 * (i2 + 1))
                    W(e, "xn1w", 16 * (C * G + j + 1))
                    dma(e, rrt[:, j2 % 2, :],
                        r1s[j2 * 128:(j2 + 1) * 128, C * TCB:(C + 1) * TCB],
                        f"r1r{i2 % 2}", 128 * TCB * 4)
            return f

        def b_xn1r_prime(C):
            def f(e):
                for j in range(2):
                    W(e, "xn1w", 16 * (C * G + j + 1))
                    dma(e, xn1r[:, j % 2, :],
                        xn1s[j * 128:(j + 1) * 128, C * TCB:(C + 1) * TCB],
                        f"xn1rd{(C * G + j) % 2}", 128 * TCB * 4)
            return f

        def b_r2(C, j, f3_bfree):
            i = C * G + j
            u = 32 * C + G + j

            def f(e):
                W(e, "Bfree", f3_bfree[0])
                W(e, f"xn1rd{i % 2}", 16 * (i // 2 + 1))
                for nm, v in r1t_consumers(u - 3):
                    W(e, nm, v)
                inc("r2", e.tensor_tensor(out=r1t[:, u % 3, :],
                                          in0=f3t[:, j % 2, :],
                                          in1=xn1r[:, j % 2, :], op=ADD))
                cost(_c_dve(TCB))
            return f

        def b_r2w(C, j):
            def f(e):
                W(e, "r2", C * G + j + 1)
                dma(e, r2s[j * 128:(j + 1) * 128, C * TCB:(C + 1) * TCB],
                    r1t[:, (32 * C + G + j) % 3, :], "r2w", 128 * TCB * 4)
            return f

        def b_r2r_loads(C):
            def f(e):
                W(e, "xn1w", 16 * G * (C + 1))
                W(e, "xn1bf", G * (C + 1))
                for j in range(2):
                    i = C * G + j
                    W(e, "r2w", 16 * (i + 1))
                    dma(e, rrt[:, j % 2, :],
                        r2s[j * 128:(j + 1) * 128, C * TCB:(C + 1) * TCB],
                        f"r2r{i % 2}", 128 * TCB * 4)
            return f

        def b_r2r_ld(C, j):
            def f(e):
                i = C * G + j
                W(e, "r2w", 16 * (i + 1))
                W(e, "outw", 16 * (i - 1))
                dma(e, rrt[:, j % 2, :],
                    r2s[j * 128:(j + 1) * 128, C * TCB:(C + 1) * TCB],
                    f"r2r{i % 2}", 128 * TCB * 4)
            return f

        def b_apply2(C, j):
            i = C * G + j

            def f(e):
                W(e, "kevac", self.ln_kevac[2 * C + 1])
                W(e, f"r2r{i % 2}", 16 * (i // 2 + 1))
                e.tensor_tensor(out=rrt[:, j % 2, :], in0=rrt[:, j % 2, :],
                                in1=mbt[:], op=SUB)
                cost(_c_dve(TCB))
                inc("outv", e.tensor_tensor(out=rrt[:, j % 2, :],
                                            in0=rrt[:, j % 2, :],
                                            in1=sbt[:], op=MUL))
                cost(_c_dve(TCB))
            return f

        def b_outw(C, j):
            def f(e):
                W(e, "outv", C * G + j + 1)
                dma(e, outT[j * 128:(j + 1) * 128, C * TCB:(C + 1) * TCB],
                    rrt[:, j % 2, :], "outw", 128 * TCB * 4)
            return f

        def fin_gp(e):
            if self.phase == "full":
                W(e, "outw", 16 * G * NCB)
            if debug and self.phase not in ("A0", "P1", "P2"):
                W(e, "ydbg", 16 * self.nca)
            if self.phase == "A0":
                W(e, "dl", G * self.nca)
                W(e, "kevac", 32 * self.nca)
            if self.phase == "P1":
                W(e, "dtdbf", 1)
                W(e, "bcbf", 1)
            if self.phase == "P2":
                W(e, "dl", G)
        w3_end_bps = {}

        opf_bfree_of = {}
        f3_bfree_of = {}

        def mk_op(C, j):
            def make():
                lf, pidx = b_wload("op", j)
                cf, bidx = b_chain(C, "op", j, pidx,
                                   ("y", 16 * (2 * C + 2)) if j == 0 else None)
                ef = b_evac(C, "op", j, bidx)
                opf_bfree_of[(C, j)] = [bidx + 1]
                return [("gp", lf), ("pe", cf), ("scalar", ef)]
            return make

        def mk_stt(C, j):
            def make():
                return [("vector", b_stt(C, j, opf_bfree_of[(C, j)])),
                        ("sync", b_r1w(C, j))]
            return make

        def mk_rbf(C, ln, j):
            def make():
                rf, ridx = b_rbf(C, ln, j)
                return [("scalar", rf),
                        ("pe", b_statmm(2 * C + ln, j, ridx))]
            return make

        def mk_ln1(C):
            def make():
                return [("vector", b_lnm_v(2 * C)),
                        ("scalar", b_lnm_s2(2 * C)),
                        ("pe", b_lnm_pe(2 * C)),
                        ("scalar", b_lnm_s3(2 * C)),
                        ("scalar", b_r1r_prime(C))]
            return make

        def mk_app1(C):
            def make():
                out = []
                for j in range(G):
                    out.append(("vector", b_apply1(C, j)))
                    out.append(("sync", b_xn1w(C, j)))
                    out.append(("scalar",
                                b_xn1bf(C, j, w3_end_bps.get(C - 1))))
                return out
            return make

        def mk_w(C, stage, j):
            def make():
                lf, pinfo = b_wload(stage, j)
                fw = ("xn1bf", G * (C + 1)) if (stage == "w1" and j == 0) \
                    else None
                cf, bidx = b_chain(C, stage, j, pinfo, fw)
                out = [("gp", lf), ("pe", cf),
                       ("scalar", b_evac(C, stage, j, bidx))]
                if stage == "w3":
                    f3_bfree_of[(C, j)] = [bidx + 1]
                if stage == "w3" and j == G - 1:
                    w3_end_bps[C] = self.bch_idx[0]
                return out
            return make

        def mk_xn1rd_prime(C):
            def make():
                return [("scalar", b_xn1r_prime(C))]
            return make

        def mk_r2(C, j):
            def make():
                return [("vector", b_r2(C, j, f3_bfree_of[(C, j)])),
                        ("sync", b_r2w(C, j))]
            return make

        def mk_ln2(C):
            def make():
                return [("vector", b_lnm_v(2 * C + 1)),
                        ("scalar", b_lnm_s2(2 * C + 1)),
                        ("pe", b_lnm_pe(2 * C + 1)),
                        ("scalar", b_lnm_s3(2 * C + 1)),
                        ("scalar", b_r2r_loads(C))]
            return make

        def mk_app2(C):
            def make():
                out = []
                for j in range(G):
                    out.append(("vector", b_apply2(C, j)))
                    out.append(("sync", b_outw(C, j)))
                    if j + 2 < G:
                        out.append(("scalar", b_r2r_ld(C, j + 2)))
                return out
            return make

        def mk_xb(C):
            def make():
                return [("scalar", b_xb_prime(C))]
            return make

        for C in (range(NCB) if self.phase == "full" else []):
            P0 = 16 * (2 * C + 2)
            at(P0 + 0, None, mk_xb(C))
            for j in range(JB):
                at(P0 + 1 + j // 2, None, mk_op(C, j))
                at(P0 + 2 + j // 2, None, mk_stt(C, j))
                at(P0 + 3 + j // 2, None, mk_rbf(C, 0, j))
            at(P0 + 12, None, mk_ln1(C))
            at(P0 + 13, None, mk_app1(C))
            for j in range(JF):
                at(P0 + 14 + j // 8, None, mk_w(C, "w1", j))
            for j in range(JF):
                at(P0 + 18 + j // 8, None, mk_w(C, "w2", j))
            at(P0 + 22, None, mk_xn1rd_prime(C))
            for j in range(G):
                at(P0 + 22 + j // 2, None, mk_w(C, "w3", j))
                at(P0 + 23 + j // 2, None, mk_r2(C, j))
                at(P0 + 24 + j // 2, None, mk_rbf(C, 1, j))
            at(P0 + 31, None, mk_ln2(C))
            at(P0 + 31, None, mk_app2(C))

        def emit_pending(P):
            for (eng2, fn) in self.dpend.pop(P, []):
                E_(eng2, fn)

        def emit_pos(P):
            for (eng, make) in self.bpos.pop(P, []):
                for item in make():
                    if len(item) == 3:
                        self.dpend.setdefault(P + item[2], []).append(
                            (item[0], item[1]))
                    else:
                        E_(item[0], item[1])

        for g in range(G):
            E_("gp", a_xload(0, g))
        E_("pe", a_mm(0))
        E_("scalar", a_evac(0))
        if self.phase not in ("P1",):
            for g in range(G):
                E_("pe", a_dtTg(0, g))
                E_("scalar", a_spg(0, g))
                E_("vector", a_dladd(0, g))
        if self.phase not in ("P1", "P2"):
            emit_bcflat(0)

        NCAr = self.nca
        for c in range(NCAr):
            for g in range(G):
                P = 16 * c + g
                emit_pending(P)
                if self.phase not in ("A0", "P1", "P2"):
                    E_("vector", a_du(c, g))
                    for q in range(NDUO):
                        E_("scalar", a_E(c, g, q))
                        E_("vector", a_scanq(c, g, q))
                    E_("vector", a_tree(c, g))
                if (self.phase not in ("A0", "P1", "P2") and c * G + g >= 1
                        and c * G + g + 2 < NCAr * G):
                    E_("gp", a_xduload(c, g))
                if self.phase not in ("A0", "P1", "P2") and c * G + g >= 8:
                    c2, g2 = divmod(c * G + g - 8, G)
                    if c2 <= NCAr - 2:
                        E_("scalar", a_hdma(c2, g2))
                emit_pos(P)
                if c + 1 < NCAr and g == 0:
                    for g2 in range(G):
                        E_("gp", a_xload(c + 1, g2))
                    E_("pe", a_mm(c + 1))
                    E_("scalar", a_evac(c + 1))
                if c + 1 < NCAr:
                    E_("pe", a_dtTg(c + 1, g))
                    E_("scalar", a_spg(c + 1, g))
                    if g >= 1:
                        E_("vector", a_dladd(c + 1, g - 1))
                if g == 0 and 1 <= c < NCAr:
                    E_("vector", a_dladd(c, G - 1))
            if debug and self.phase not in ("A0", "P1", "P2"):
                E_("gp", a_ydbg(c))
            if c + 1 < NCAr:
                emit_bcflat(c + 1)
        while self.bpos or self.dpend:
            k = min(set(self.bpos) | set(self.dpend))
            emit_pending(k)
            emit_pos(k)
        if self.snap:
            def snap_dump(e):
                W(e, "scan", NCAr * G * NDUO)
                dma(e, r1s[0:1024, :].rearrange("(a p) m -> p a m", p=128),
                    h1[:].rearrange("p (x y) b -> p x (y b)", y=4), "ydbg",
                    128 * 16384 * 2)
                dma(e, r1s[1024:1152, 0:256], Hcar[:].rearrange("p n d -> p (n d)"), "ydbg", 128 * 256 * 2)
                dma(e, r1s[1152:1280, 0:256], HcarD[:].rearrange("p n d -> p (n d)"), "ydbg", 128 * 256 * 2)
            E_("gp", snap_dump)
        E_("gp", fin_gp)

        class _MockInstr:
            def then_inc(self, sem, n):
                return self

        class _MockEng:
            def __getattr__(self, name):
                def fn(*a, **k):
                    return _MockInstr()
                return fn

        mock = _MockEng()
        self.mode = "plan"
        for eng, fn in self.ev:
            self.cur = eng
            fn(mock)
        self.CNT = {n: 0 for n in self.CNT}
        self.kp_idx[0] = 0
        self.ke_idx[0] = 0
        self.mode = "emit"

        blk = es.enter_context(nc.Block())
        b = self

        @blk.sync
        def _(e):
            b.cur = "sync"
            for eng, fn in b.ev:
                if eng == "sync":
                    fn(e)

        @blk.scalar
        def _(e):
            b.cur = "scalar"
            for eng, fn in b.ev:
                if eng == "scalar":
                    fn(e)

        @blk.vector
        def _(e):
            b.cur = "vector"
            for eng, fn in b.ev:
                if eng == "vector":
                    fn(e)

        @blk.tensor
        def _(e):
            b.cur = "pe"
            for eng, fn in b.ev:
                if eng == "pe":
                    fn(e)

        @blk.gpsimd
        def _(e):
            b.cur = "gp"
            for eng, fn in b.ev:
                if eng == "gp":
                    fn(e)

        es.close()
        return nc

    def check_schedule(self, dma_GBps=300.0, verbose=True):
        sems = {n: 0 for n in self.CNT}
        streams = {k: list(v) for k, v in self.shadow.items()}
        ptr = {k: 0 for k in streams}
        clock = {k: 0.0 for k in streams}
        pending = []
        busy = {k: 0.0 for k in streams}
        import heapq
        while True:
            progressed = False
            for eng in streams:
                ops = streams[eng]
                while ptr[eng] < len(ops):
                    op = ops[ptr[eng]]
                    if op[0] == "w":
                        _, sem, val = op
                        if sems[sem] >= val:
                            ptr[eng] += 1
                            progressed = True
                            continue
                        due = [p for p in pending if p[1] == sem]
                        due.sort()
                        sat = False
                        cnt = sems[sem]
                        for (t, s, n) in due:
                            cnt += n
                            if cnt >= val:
                                tt = max(t, clock[eng])
                                clock[eng] = tt
                                for p in list(pending):
                                    if p[0] <= t:
                                        sems[p[1]] += p[2]
                                        pending.remove(p)
                                sat = sems[sem] >= val
                                break
                        if sat:
                            ptr[eng] += 1
                            progressed = True
                            continue
                        break
                    elif op[0] == "i":
                        _, sem, tot, n = op
                        sems[sem] += n
                        ptr[eng] += 1
                        progressed = True
                    elif op[0] == "c":
                        clock[eng] += op[1]
                        busy[eng] += op[1]
                        ptr[eng] += 1
                        progressed = True
                    elif op[0] == "d":
                        _, sem, tot, nbytes = op
                        clock[eng] += 500
                        t_done = clock[eng] + 1000 + nbytes / dma_GBps
                        pending.append((t_done, sem, 16))
                        ptr[eng] += 1
                        progressed = True
            if all(ptr[k] >= len(streams[k]) for k in streams):
                break
            if not progressed:
                if pending:
                    t0 = min(p[0] for p in pending)
                    for p in list(pending):
                        if p[0] <= t0:
                            sems[p[1]] += p[2]
                            pending.remove(p)
                    for eng in clock:
                        if ptr[eng] < len(streams[eng]):
                            clock[eng] = max(clock[eng], t0)
                    continue
                info = {}
                for eng in streams:
                    if ptr[eng] < len(streams[eng]):
                        info[eng] = streams[eng][ptr[eng]]
                raise RuntimeError(f"DEADLOCK: stalled at {info}; "
                                   f"sems={ {k: v for k, v in sems.items()} }")
        if verbose:
            tot = max(clock.values())
            print(f"sim total = {tot/1e3:.0f} us; busy: " +
                   ", ".join(f"{k}={busy[k]/1e3:.0f}" for k in busy))
        for eng, ops in self.shadow.items():
            for op in ops:
                if op[0] == "w":
                    assert self.CNT[op[1]] >= op[2], \
                        f"unreachable wait {op} on {eng}"
        return max(clock.values())


def host_inputs(inp, core):
    x = np.ascontiguousarray(np.asarray(inp["L0_attn_hidden_states"])[core].T)
    xpw = np.asarray(inp["L0_attn_x_proj_weight"])
    dtw = np.asarray(inp["L0_attn_dt_proj_weight"])

    def gmaj(v, width):
        return np.ascontiguousarray(np.asarray(v).reshape(width, 128).T)

    def panels(w):
        w = np.asarray(w)
        p = w.T.reshape(w.shape[1], -1, 128).transpose(1, 0, 2)
        n, din, _ = p.shape
        return np.ascontiguousarray(
            p.reshape(n, din // 128, 128, 128).transpose(0, 2, 1, 3)
            .reshape(n, 128, din))

    alog = np.asarray(inp["L0_attn_A_log"])
    return {
        "xT": x,
        "xpdt": np.ascontiguousarray(xpw[:128, :].T),
        "xpbc": np.ascontiguousarray(xpw[128:160, :].T),
        "dtwT": np.ascontiguousarray(dtw.T),
        "dtb_g": gmaj(inp["L0_attn_dt_proj_bias"], G),
        "alog_g": np.ascontiguousarray(
            alog.reshape(G, 128, NST).transpose(1, 0, 2).reshape(128,
                                                                 G * NST)),
        "dd_g": gmaj(inp["L0_attn_D"], G),
        "opb_g": gmaj(inp["L0_attn_out_proj_bias"], G),
        "b1_g": gmaj(inp["L0_ffn_b1"], JF),
        "b2_g": gmaj(inp["L0_ffn_b2"], JF),
        "b3_g": gmaj(inp["L0_ffn_b3"], G),
        "wop": panels(inp["L0_attn_out_proj_weight"]),
        "w1p": panels(inp["L0_ffn_W1"]),
        "w2p": panels(inp["L0_ffn_W2"]),
        "w3p": panels(inp["L0_ffn_W3"]),
    }


if __name__ == "__main__":
    b = Builder(debug=True)
    nc = b.build()
    t = b.check_schedule()
    ni = sum(len(v) for v in b.shadow.values())
    print(f"build ok; shadow items={ni}")


from concourse.bass_utils import run_bass_kernel_spmd as _run_spmd


def kernel(**inputs):
    b = Builder(debug=False, phase="full")
    nc = b.build()
    in_maps = [host_inputs(inputs, core) for core in range(8)]
    res = _run_spmd(nc, in_maps, core_ids=list(range(8)))
    out = np.stack([np.asarray(res.results[c]["outT"]).T
                    for c in range(8)], 0)
    return np.ascontiguousarray(out, dtype=np.float32)
